# version 18
# baseline (speedup 1.0000x reference)
import math
from contextlib import ExitStack

import numpy as np
import ml_dtypes

import concourse.bass as bass
import concourse.mybir as mybir
from concourse.bass_utils import run_bass_kernel_spmd

F32 = mybir.dt.float32
BF16 = mybir.dt.bfloat16
AF = mybir.ActivationFunctionType
ALU = mybir.AluOpType
AX = mybir.AxisListType

P = 128
D = 2048
NH = 16
DH = 128
NDC = 16
NT = 16
SEQL = 2048
NSEQ = 2
NCORES = 8

SLOPES = [2.0 ** (-0.5 * (h + 1)) for h in range(NH)]
DK = [min(15, int(136.0 / (128.0 * s)) + 1) for s in SLOPES]
EXACT_HEADS = (0, 1)
PERQT_HEADS = (2, 3, 4, 5)
GROUP_HEADS = tuple(range(6, 16))
NEG = -1.0e30

def _prows(dk):
    offs = []
    off = 0
    for c in range(NT):
        q1 = min(c + dk, NT - 1)
        offs.append((off, c, q1))
        off += (q1 - c + 1) * P
    return offs, off

PROW_MAX = max(_prows(dk)[1] for dk in DK)

CST_CAUSALT = 0
CST_CAUSALTM = 128
_t0_off = {}
_c = 256
for _h in EXACT_HEADS:
    _t0_off[_h] = _c
    _c += (DK[_h] + 1) * 2
_t1_off = {}
for _h in PERQT_HEADS:
    _t1_off[_h] = _c
    _c += DK[_h] + 1
_t2_off = {}
for _h in GROUP_HEADS:
    _t2_off[_h] = _c
    _c += 16
CST_IOTA8 = _c; _c += 8
CST_EPS = _c; _c += 1
CST_MHALF = _c; _c += 1
CST_NCOL = _c

VSLOT = 144


class Ctr:
    def __init__(self, h):
        self.h = h
        self.n = 0


class Prog:
    ENGS = ("sync", "gpsimd", "vector", "scalar", "tensor")

    def __init__(self, nc):
        self.nc = nc
        self.cur = None

    def new_block(self):
        self.cur = {e: [] for e in self.ENGS}

    def op(self, eng, f, inc=None, amt=1, waits=()):
        tok = None
        if inc is not None:
            inc.n += amt
            tok = inc.n
        wl = [(c.h, v) for (c, v) in waits if v and v > 0]

        def th(e, f=f, wl=wl, inc=inc, amt=amt):
            for (h, v) in wl:
                e.wait_ge(h, v)
            ins = f(e)
            if inc is not None:
                ins.then_inc(inc.h, amt)
        self.cur[eng].append(th)
        return tok

    def wait(self, eng, ctr, val):
        if val and val > 0:
            self.cur[eng].append(lambda e, h=ctr.h, v=val: e.wait_ge(h, v))

    def run_block(self, blk):
        for name in self.ENGS:
            lst = self.cur[name]
            if not lst:
                continue
            getattr(blk, name)(lambda e, lst=lst: [t(e) for t in lst])
        self.cur = None


def build_program(debug=False):
    nc = bass.Bass("TRN2", target_bir_lowering=False)

    x_d = nc.dram_tensor("x", [NSEQ, NT, P, D], F32, kind="ExternalInput")
    wq_d = nc.dram_tensor("wq", [NH, P, NDC, DH], F32, kind="ExternalInput")
    wk_d = nc.dram_tensor("wk", [NH, P, NDC, DH], F32, kind="ExternalInput")
    wv_d = nc.dram_tensor("wv", [NH, P, NDC, DH], F32, kind="ExternalInput")
    wo_d = nc.dram_tensor("wo", [NDC, P, D], F32, kind="ExternalInput")
    wf_d = nc.dram_tensor("wf", [NDC, P, D], F32, kind="ExternalInput")
    fb_d = nc.dram_tensor("fb", [1, D], F32, kind="ExternalInput")
    la_d = nc.dram_tensor("la", [NDC, P, 128], F32, kind="ExternalInput")
    lb_d = nc.dram_tensor("lb", [128, D], F32, kind="ExternalInput")
    gw_d = nc.dram_tensor("gw", [NDC, P, 8], F32, kind="ExternalInput")
    gb_d = nc.dram_tensor("gb", [1, 8], F32, kind="ExternalInput")
    arep_d = nc.dram_tensor("arep", [2, P, 512], F32, kind="ExternalInput")
    cst_d = nc.dram_tensor("cst", [P, CST_NCOL], F32, kind="ExternalInput")
    e8_d = nc.dram_tensor("e8", [8, 128], F32, kind="ExternalInput")
    idf_d = nc.dram_tensor("idf", [P, P], F32, kind="ExternalInput")
    x2_d = nc.dram_tensor("x2d", [NT, P, D], F32, kind="Internal")
    out_d = nc.dram_tensor("out", [NSEQ, NT, P, D], F32, kind="ExternalOutput")
    dbg = None
    if debug:
        dbg = dict(
            xnT=nc.dram_tensor("dxnT", [P, NDC * SEQL], BF16,
                               kind="ExternalOutput"),
            zT=nc.dram_tensor("dzT", [P, NH * SEQL], BF16,
                              kind="ExternalOutput"),
            x2=nc.dram_tensor("dx2", [NT, P, D], F32, kind="ExternalOutput"))

    dram = dict(x=x_d, wq=wq_d, wk=wk_d, wv=wv_d, wo=wo_d, wf=wf_d, fb=fb_d,
                la=la_d, lb=lb_d, gw=gw_d, gb=gb_d, arep=arep_d, cst=cst_d,
                e8=e8_d, idf=idf_d, x2=x2_d, out=out_d)

    es = ExitStack()
    with es:
        sem = dict(
            pe=Ctr(es.enter_context(nc.semaphore("s_pe"))),
            dve=Ctr(es.enter_context(nc.semaphore("s_dve"))),
            act=Ctr(es.enter_context(nc.semaphore("s_act"))),
            pool=Ctr(es.enter_context(nc.semaphore("s_pool"))),
            dx=Ctr(es.enter_context(nc.semaphore("s_dx"))),
            dw=Ctr(es.enter_context(nc.semaphore("s_dw"))),
            dt=Ctr(es.enter_context(nc.semaphore("s_dt"))),
            do=Ctr(es.enter_context(nc.semaphore("s_do"))),
            dl=Ctr(es.enter_context(nc.semaphore("s_dl"))),
            ds=Ctr(es.enter_context(nc.semaphore("s_ds"))),
            dc=Ctr(es.enter_context(nc.semaphore("s_dc"))),
            wg=[Ctr(es.enter_context(nc.semaphore(f"s_wg{i}")))
                for i in range(8)],
            dls=[Ctr(es.enter_context(nc.semaphore(f"s_dls{i}")))
                 for i in range(5)],
            dxa=[Ctr(es.enter_context(nc.semaphore(f"s_dxa{i}")))
                 for i in range(4)],
            bxl=[Ctr(es.enter_context(nc.semaphore(f"s_bxl{i}")))
                 for i in range(2)],
            cxl=[Ctr(es.enter_context(nc.semaphore(f"s_cxl{i}")))
                 for i in range(2)],
            wpre=[Ctr(es.enter_context(nc.semaphore(f"s_wpre{i}")))
                  for i in range(3)],
            dob=[Ctr(es.enter_context(nc.semaphore(f"s_dob{i}")))
                 for i in range(2)],
            dsb=[Ctr(es.enter_context(nc.semaphore(f"s_dsb{i}")))
                 for i in range(2)],
        )
        pg = Prog(nc)

        cst = es.enter_context(nc.sbuf_tensor("cstt", [P, CST_NCOL], F32))
        e8 = es.enter_context(nc.sbuf_tensor("e8t", [8, 128], F32))
        idf = es.enter_context(nc.sbuf_tensor("idft", [P, P], F32))
        idb = es.enter_context(nc.sbuf_tensor("idbt", [P, P], BF16))
        onesb = es.enter_context(nc.sbuf_tensor("onesbt", [1, P], BF16))
        sb = dict(cst=cst, e8=e8, idf=idf, idb=idb, onesb=onesb)

        nc.const_aps.register(F32, 1e-5, cst[:, CST_EPS:CST_EPS + 1])
        nc.const_aps.register(F32, -0.5, cst[:, CST_MHALF:CST_MHALF + 1])

        pg.new_block()
        pg.op("sync", lambda e: e.dma_start(cst[:, :], cst_d.ap()),
              inc=sem["dx"], amt=16)
        pg.op("sync", lambda e: e.dma_start(e8[:, :], e8_d.ap()),
              inc=sem["dx"], amt=16)
        pg.op("sync", lambda e: e.dma_start(idf[:, :], idf_d.ap()),
              inc=sem["dx"], amt=16)
        pg.op("gpsimd", lambda e: e.dma_start(idb[:, :], idf_d.ap()),
              inc=sem["dw"], amt=16)
        pg.op("gpsimd", lambda e: e.memset(onesb[:, :], 1.0), inc=sem["pool"])
        pg.wait("sync", sem["dx"], sem["dx"].n)
        pg.wait("gpsimd", sem["dw"], sem["dw"].n)
        with nc.Block() as blk:
            pg.run_block(blk)

        zT = es.enter_context(nc.sbuf_tensor("zTt", [P, NH * SEQL], BF16))
        sb["zT"] = zT

        dram["dbg"] = dbg
        for s in range(NSEQ):
            _build_seq(nc, pg, s, dram, sb, sem)

        pg.new_block()
        for _p in range(2):
            pg.wait("sync", sem["dob"][_p], sem["dob"][_p].n)
        with nc.Block() as blk:
            pg.run_block(blk)

    return nc


def _build_seq(nc, pg, s, dram, sb, sem):
    s_pe, s_dve, s_act, s_pool = sem["pe"], sem["dve"], sem["act"], sem["pool"]
    s_dx, s_dw, s_dt, s_do = sem["dx"], sem["dw"], sem["dt"], sem["do"]
    s_dl, s_ds, s_dc = sem["dl"], sem["ds"], sem["dc"]
    x_d, out_d, x2_d = dram["x"], dram["out"], dram["x2"]
    cst, idb, zT = sb["cst"], sb["idb"], sb["zT"]

    with ExitStack() as seq_es:
        attn_es = ExitStack()
        xnT = attn_es.enter_context(
            nc.sbuf_tensor(f"xnT{s}", [P, NDC * SEQL], BF16))
        wq_sb = attn_es.enter_context(
            nc.sbuf_tensor(f"wq{s}", [P, NDC * DH], BF16))
        wk_sb = attn_es.enter_context(
            nc.sbuf_tensor(f"wk{s}", [P, NDC * DH], BF16))
        wv_sb = attn_es.enter_context(
            nc.sbuf_tensor(f"wv{s}", [P, NDC * DH], BF16))
        wtok = [[0, 0, 0] for _ in range(NH + 1)]

        with ExitStack() as a2es:
            bufs = dict(
                probsT=a2es.enter_context(
                    nc.sbuf_tensor(f"prT{s}", [P, PROW_MAX], BF16)),
                QT=a2es.enter_context(nc.sbuf_tensor(f"QT{s}", [P, SEQL], BF16)),
                KT=a2es.enter_context(nc.sbuf_tensor(f"KT{s}", [P, SEQL], BF16)),
                vTs=a2es.enter_context(nc.sbuf_tensor(f"vT{s}", [P, SEQL], BF16)),
                vaug=a2es.enter_context(
                    nc.sbuf_tensor(f"va{s}", [P, NT * VSLOT], BF16)),
                spanb=[a2es.enter_context(
                    nc.sbuf_tensor(f"sp{s}{i}", [P, 512], F32))
                    for i in range(3)],
                diagb=a2es.enter_context(
                    nc.sbuf_tensor(f"dg{s}", [P, 128], F32)),
                zcol=a2es.enter_context(nc.sbuf_tensor(f"zc{s}", [P, 8], F32)),
                ztm=[a2es.enter_context(
                    nc.sbuf_tensor(f"zt{s}{i}", [P, 128], BF16))
                    for i in range(2)],
                ppb=[a2es.enter_context(
                    nc.psum_tensor(f"pp{s}{i}", [P, 512], F32))
                    for i in range(2)],
                psc=[a2es.enter_context(
                    nc.psum_tensor(f"ps{s}{i}", [P, 512], F32))
                    for i in range(3)],
                pz=[a2es.enter_context(
                    nc.psum_tensor(f"pz{s}{i}", [P, 129], F32))
                    for i in range(2)],
                ptp=a2es.enter_context(
                    nc.psum_tensor(f"ptp{s}", [P, 512], BF16)),
                xnT=xnT, zT=zT, cst=cst, wq=wq_sb, wk=wk_sb, wv=wv_sb,
                idb=idb)

            pg.new_block()
            pg.op("gpsimd", lambda e: e.memset(
                bass.AP(bufs["vaug"], 128,
                        [[NT * VSLOT, P], [VSLOT, NT], [1, 1]]), 1.0),
                inc=s_pool)

            st = dict(proj_evac=[0, 0], span_al=[0, 0, 0],
                      span_exp=[0, 0, 0], pz_guard=[0, 0],
                      sc_guard=[0, 0, 0], vaug_ones=s_pool.n, vaug_reader=0,
                      vts_tok=0, arep_tok=0, arep_user=0,
                      diag_exp=0, diag_dve=0, span_dve=[0, 0, 0],
                      ptb_guard=[0, 0], probsT_guard=0, zc3_user=0,
                      ptp_copy=0, ztr_tok_pe=[0, 0], pend_z=[])

            pf32 = bufs["probsT"].bitcast(F32)
            xbuf = [pf32[:, 0:2048], pf32[:, 2048:4096]]
            cc = pf32[:, 4096:4112]
            xnb = [bufs["probsT"][:, 12288:14336],
                   bufs["probsT"][:, 14336:16384]]
            for w_d, w_sb in ((dram["wv"], wv_sb), (dram["wq"], wq_sb),
                              (dram["wk"], wk_sb)):
                wi = {id(wq_sb): 0, id(wk_sb): 1, id(wv_sb): 2}[id(w_sb)]
                wtok[0][wi] = pg.op(
                    "gpsimd",
                    lambda e, w_d=w_d, w_sb=w_sb: e.dma_start(
                        w_sb[:, :],
                        w_d.ap()[0].rearrange("p c o -> p (c o)")),
                    inc=sem["wpre"][wi], amt=16)
            apply_tok = [0, 0]
            xnb_guard = [0, 0]
            cp_tok = [[0] * 4 for _ in range(NT)]
            for i in range(NT):
                bb = i % 2
                ld = pg.op("sync",
                           lambda e, i=i, bb=bb: e.dma_start(
                               xbuf[bb], x_d.ap()[s][i]),
                           inc=sem["dxa"][bb], amt=16,
                           waits=[(s_pool, apply_tok[bb])])
                t8 = _plan_ln(pg, xbuf[bb], xnb[bb], xnb[bb], cc, 8 * bb,
                              sem, (sem["dxa"][bb], ld),
                              [(s_pe, xnb_guard[bb])], eng="gpsimd")
                apply_tok[bb] = t8
                last_t = 0
                for g in range(4):
                    for j in range(4):
                        dc = g * 4 + j
                        w = []
                        if j == 0:
                            w = [(s_pool, t8), (s_dve, st["ptp_copy"])]
                        last_t = pg.op(
                            "tensor",
                            lambda e, bb=bb, dc=dc, j=j: e.transpose(
                                bufs["ptp"][:, j * 128:(j + 1) * 128],
                                xnb[bb][:, dc * 128:(dc + 1) * 128],
                                sb["idb"][:, :]),
                            inc=s_pe, waits=w)
                    ct = pg.op("vector",
                               lambda e, i=i, g=g: e.tensor_copy(
                                   bass.AP(xnT, (4 * g) * SEQL + i * 128,
                                           [[NDC * SEQL, P], [SEQL, 4],
                                            [1, 128]]),
                                   bufs["ptp"][:, 0:512]),
                               inc=s_dve, waits=[(s_pe, last_t)])
                    st["ptp_copy"] = ct
                    cp_tok[i][g] = ct
                xnb_guard[bb] = last_t
            st["xnT_cp"] = cp_tok

            for h in range(NH):
                _plan_head(pg, s, h, bufs, dram, sem, st, wtok)

            pg.wait("sync", s_dt, s_dt.n)
            pg.wait("sync", s_dx, s_dx.n)
            for _w in range(3):
                pg.wait("gpsimd", sem["wpre"][_w], sem["wpre"][_w].n)
            with nc.Block() as blk:
                pg.run_block(blk)

            if s == 0 and dram.get("dbg"):
                pg.new_block()
                pg.op("sync", lambda e: e.dma_start(
                    dram["dbg"]["zT"].ap(), zT[:, :]), inc=s_dx, amt=16)
                pg.wait("sync", s_dx, s_dx.n)
                with nc.Block() as blk:
                    pg.run_block(blk)

        attn_es.close()
        bces = ExitStack()
        cx2b = [bces.enter_context(nc.sbuf_tensor(f"c2{s}{i}", [P, D], F32))
                for i in range(2)]
        cxnb = [bces.enter_context(
            nc.sbuf_tensor(f"cn{s}{i}", [P, D], BF16)) for i in range(2)]
        ccc = bces.enter_context(nc.sbuf_tensor(f"cc{s}", [P, 16], F32))
        with ExitStack() as bes:

            woT = bes.enter_context(
                nc.sbuf_tensor(f"wo{s}", [P, NDC * D], BF16))
            xb = [bes.enter_context(nc.sbuf_tensor(f"bx{s}{i}", [P, D], F32))
                  for i in range(2)]
            x2b = [bes.enter_context(nc.sbuf_tensor(f"b2{s}{i}", [P, D], F32))
                   for i in range(2)]
            po = [bes.enter_context(nc.psum_tensor(f"bp{s}{i}", [P, 512], F32))
                  for i in range(8)]

            pg.new_block()
            s_wg = sem["wg"]
            wo_tok = []
            for c in range(NDC):
                t = pg.op("gpsimd",
                          lambda e, c=c: e.dma_start(
                              woT[:, c * D:(c + 1) * D], dram["wo"].ap()[c]),
                          inc=s_wg[c // 2], amt=16)
                wo_tok.append(t)
            evac_tok = [0, 0]
            store_tok = [0, 0]
            prefix = []
            for i in range(NT):
                b = i % 2
                ld = pg.op("sync",
                           lambda e, i=i, b=b: e.dma_start(
                               xb[b][:, :], x_d.ap()[s][i]),
                           inc=sem["bxl"][b], amt=16,
                           waits=[(s_dve, evac_tok[b])])
                mm_tok = [0] * 4
                for c in range(NDC):
                    for og in range(4):
                        w = []
                        if og == 0 and c % 2 == 0:
                            w.append((s_wg[c // 2], wo_tok[(c // 2) * 2 + 1]))
                        if c == 0:
                            w.append((s_dve, evac_tok[b]))
                        mm_tok[og] = pg.op(
                            "tensor",
                            lambda e, i=i, b=b, c=c, og=og: e.matmul(
                                po[4 * b + og][:, :],
                                zT[:, c * SEQL + i * 128:
                                   c * SEQL + (i + 1) * 128],
                                woT[:, c * D + og * 512:
                                    c * D + (og + 1) * 512],
                                start=(c == 0), stop=(c == NDC - 1)),
                            inc=s_pe, waits=w)
                etoks = []
                for og in range(4):
                    t = pg.op("vector",
                              lambda e, b=b, og=og: e.tensor_tensor(
                                  x2b[b][:, og * 512:(og + 1) * 512],
                                  po[4 * b + og][:, :],
                                  xb[b][:, og * 512:(og + 1) * 512], ALU.add),
                              inc=s_dve,
                              waits=[(s_pe, mm_tok[og]), (sem["bxl"][b], ld),
                                     (sem["dsb"][b], store_tok[b])])
                    etoks.append(t)
                evac_tok[b] = etoks[-1]
                store_tok[b] = pg.op("gpsimd",
                                     lambda e, i=i, b=b: e.dma_start(
                                         x2_d.ap()[i], x2b[b][:, :]),
                                     inc=sem["dsb"][b], amt=16,
                                     waits=[(s_dve, t) for t in etoks])
                if s == 0 and dram.get("dbg"):
                    pg.op("sync",
                          lambda e, i=i, b=b: e.dma_start(
                              dram["dbg"]["x2"].ap()[i], x2b[b][:, :]),
                          inc=s_dx, amt=16,
                          waits=[(s_dve, t) for t in etoks])
                if i == 1:
                    for t0 in range(2):
                        pld = pg.op("scalar",
                                    lambda e, t0=t0: e.dma_start(
                                        cx2b[t0][:, :], x2_d.ap()[t0]),
                                    inc=sem["cxl"][t0], amt=16,
                                    waits=[(sem["dsb"][t0], store_tok[t0])])
                        pta = _plan_ln(pg, cx2b[t0], cxnb[t0], cxnb[t0],
                                       ccc, 8 * t0, sem,
                                       (sem["cxl"][t0], pld),
                                       [], eng="gpsimd")
                        prefix.append((pld, pta))
            for _p in range(2):
                pg.wait("sync", sem["bxl"][_p], sem["bxl"][_p].n)
                pg.wait("gpsimd", sem["dsb"][_p], sem["dsb"][_p].n)
            for _g in range(8):
                pg.wait("gpsimd", s_wg[_g], s_wg[_g].n)
            with nc.Block() as blk:
                pg.run_block(blk)


        with ExitStack() as ces:
            cb = dict(
                wfT=ces.enter_context(
                    nc.sbuf_tensor(f"wf{s}", [P, NDC * D], BF16)),
                at=ces.enter_context(
                    nc.sbuf_tensor(f"at{s}", [P, NDC * 128], BF16)),
                bt=ces.enter_context(nc.sbuf_tensor(f"bt{s}", [P, D], BF16)),
                gwt=ces.enter_context(
                    nc.sbuf_tensor(f"gw{s}", [P, NDC * 8], BF16)),
                fbr=ces.enter_context(nc.sbuf_tensor(f"fb{s}", [1, D], BF16)),
                gbr=ces.enter_context(nc.sbuf_tensor(f"gb{s}", [1, 8], BF16)),
                x2b=cx2b,
                outb=[ces.enter_context(
                    nc.sbuf_tensor(f"co{s}{i}", [P, D], F32))
                    for i in range(2)],
                xnb=cxnb,
                xnT2=[ces.enter_context(
                    nc.sbuf_tensor(f"cT{s}{i}", [P, NDC * 128], BF16))
                    for i in range(2)],
                cc=ccc,
                gl=ces.enter_context(nc.sbuf_tensor(f"cg{s}", [P, 8], F32)),
                gsc=ces.enter_context(nc.sbuf_tensor(f"cq{s}", [P, 56], F32)),
                wts=ces.enter_context(nc.sbuf_tensor(f"cw{s}", [8, 128], F32)),
                xa=ces.enter_context(nc.sbuf_tensor(f"cx{s}", [P, 128], F32)),
                wrs=ces.enter_context(nc.sbuf_tensor(f"cr{s}", [P, 128], F32)),
                wxa=ces.enter_context(nc.sbuf_tensor(f"cm{s}", [P, 128], BF16)),
                pb=[ces.enter_context(
                    nc.psum_tensor(f"cb{s}{i}", [P, 512], F32))
                    for i in range(2)],
                pxa=ces.enter_context(nc.psum_tensor(f"cpx{s}", [P, 128], F32)),
                pgl=ces.enter_context(nc.psum_tensor(f"cpg{s}", [P, 8], F32)),
                pwt=ces.enter_context(nc.psum_tensor(f"cpw{s}", [8, 128], F32)),
                pwr=ces.enter_context(nc.psum_tensor(f"cpr{s}", [P, 128], F32)),
                ptc=[ces.enter_context(
                    nc.psum_tensor(f"ptc{s}{i}", [P, 512], BF16))
                    for i in range(2)],
            )
            pg.new_block()
            lora_tok = []
            for li, (src, dst) in enumerate((
                    (dram["gw"].ap().rearrange("c p g -> p c g"),
                     cb["gwt"][:, :].rearrange("p (c g) -> p c g", g=8)),
                    (dram["gb"].ap(), cb["gbr"][:, :]),
                    (dram["la"].ap().rearrange("c p r -> p c r"),
                     cb["at"][:, :].rearrange("p (c r) -> p c r", r=128)),
                    (dram["lb"].ap(), cb["bt"][:, :]),
                    (dram["fb"].ap(), cb["fbr"][:, :]))):
                t = pg.op("gpsimd",
                          lambda e, src=src, dst=dst: e.dma_start(dst, src),
                          inc=sem["dls"][li], amt=16)
                lora_tok.append((sem["dls"][li], t))
            wf_tok = []
            for c in range(NDC):
                t = pg.op("gpsimd",
                          lambda e, c=c: e.dma_start(
                              cb["wfT"][:, c * D:(c + 1) * D],
                              dram["wf"].ap()[c]),
                          inc=sem["wg"][c // 2], amt=16)
                wf_tok.append(t)

            _plan_ffn(pg, s, cb, sb, sem, wf_tok, lora_tok, x2_d, out_d,
                      prefix)

            for _p in range(2):
                pg.wait("scalar", sem["cxl"][_p], sem["cxl"][_p].n)
                pg.wait("sync", sem["dob"][_p], sem["dob"][_p].n)
            for _g in range(8):
                pg.wait("gpsimd", sem["wg"][_g], sem["wg"][_g].n)
            for _l in range(5):
                pg.wait("gpsimd", sem["dls"][_l], sem["dls"][_l].n)
            with nc.Block() as blk:
                pg.run_block(blk)
        bces.close()


def _plan_ln(pg, xin, xout_bf, sqb, cc, c0, sem, ld, apply_extra,
             eng="vector"):
    s_dve = sem["dve"] if eng == "vector" else sem["pool"]
    s_act = sem["act"]
    ldc, ldt = ld
    t_sq = pg.op("scalar",
                 lambda e: e.activation(sqb[:, :], xin[:, :], AF.Square,
                                        accum_out=cc[:, c0:c0 + 1]),
                 inc=s_act, waits=[(ldc, ldt)] + list(apply_extra))
    t_sum = pg.op("scalar",
                  lambda e: e.activation(sqb[:, :], xin[:, :], AF.Identity,
                                         accum_out=cc[:, c0 + 1:c0 + 2]),
                  inc=s_act, waits=[(ldc, ldt), (s_act, t_sq)])
    t1 = pg.op(eng,
               lambda e: e.tensor_scalar(cc[:, c0 + 2:c0 + 3],
                                         cc[:, c0 + 1:c0 + 2],
                                         1.0 / D, None, ALU.mult),
               inc=s_dve, waits=[(s_act, t_sum)])
    t2 = pg.op(eng,
               lambda e: e.tensor_scalar(cc[:, c0 + 3:c0 + 4],
                                         cc[:, c0:c0 + 1],
                                         1.0 / D, None, ALU.mult),
               inc=s_dve, waits=[(s_act, t_sq), (s_dve, t1)])
    t3 = pg.op(eng,
               lambda e: e.tensor_tensor(cc[:, c0 + 4:c0 + 5],
                                         cc[:, c0 + 2:c0 + 3],
                                         cc[:, c0 + 2:c0 + 3], ALU.mult),
               inc=s_dve, waits=[(s_dve, t2)])
    t4 = pg.op(eng,
               lambda e: e.tensor_tensor(cc[:, c0 + 5:c0 + 6],
                                         cc[:, c0 + 3:c0 + 4],
                                         cc[:, c0 + 4:c0 + 5], ALU.subtract),
               inc=s_dve, waits=[(s_dve, t3)])
    t5 = pg.op("scalar",
               lambda e: e.activation(cc[:, c0 + 4:c0 + 5],
                                      cc[:, c0 + 5:c0 + 6], AF.Ln, bias=1e-5),
               inc=s_act, waits=[(s_dve, t4)])
    t6 = pg.op("scalar",
               lambda e: e.activation(cc[:, c0 + 6:c0 + 7],
                                      cc[:, c0 + 4:c0 + 5], AF.Exp, scale=-0.5),
               inc=s_act, waits=[(s_act, t5)])
    t7 = pg.op(eng,
               lambda e: e.tensor_scalar(cc[:, c0 + 7:c0 + 8],
                                         cc[:, c0 + 2:c0 + 3],
                                         cc[:, c0 + 6:c0 + 7], -1.0,
                                         ALU.mult, ALU.mult),
               inc=s_dve, waits=[(s_act, t6)])
    t8 = pg.op(eng,
               lambda e: e.tensor_scalar(xout_bf[:, :], xin[:, :],
                                         cc[:, c0 + 6:c0 + 7],
                                         cc[:, c0 + 7:c0 + 8],
                                         ALU.mult, ALU.add),
               inc=s_dve,
               waits=[(s_dve, t7)] + list(apply_extra))
    return t8


def _plan_head(pg, s, h, b, dram, sem, st, wtok):
    s_pe, s_dve, s_act, s_pool = sem["pe"], sem["dve"], sem["act"], sem["pool"]
    s_dx, s_dw, s_dt = sem["dx"], sem["dw"], sem["dt"]
    xnT, zT, cst = b["xnT"], b["zT"], b["cst"]
    probsT, QT, KT, vTs, vaug = (b["probsT"], b["QT"], b["KT"], b["vTs"],
                                 b["vaug"])
    spanb, diagb = b["spanb"], b["diagb"]
    zcol, ztm = b["zcol"], b["ztm"]
    ppb, psc, pz, ptp, idb = b["ppb"], b["psc"], b["pz"], b["ptp"], b["idb"]
    dk = DK[h]
    prow, _ = _prows(dk)
    off = {c: prow[c][0] for c in range(NT)}

    def proj(w_sb, dst, wi, scale=None):
        mm_last = [0] * 4
        ev = [0] * 4
        for tg in range(4):
            bk = tg % 2
            for dc in range(NDC):
                w = []
                if h == 0:
                    w.append((s_dve, st["xnT_cp"][4 * tg + 3][dc // 4]))
                if dc == 0:
                    if tg == 0:
                        w.append((sem["wpre"][wi], wtok[h][wi]))
                    w.append((s_dve, st["proj_evac"][bk]))
                mm_last[tg] = pg.op(
                    "tensor",
                    lambda e, dc=dc, tg=tg, bk=bk, w_sb=w_sb: e.matmul(
                        ppb[bk][:, :],
                        w_sb[:, dc * DH:(dc + 1) * DH],
                        xnT[:, dc * SEQL + tg * 512:
                            dc * SEQL + (tg + 1) * 512],
                        start=(dc == 0), stop=(dc == NDC - 1)),
                    inc=s_pe, waits=w)
            bk = tg % 2
            if scale is None:
                ev[tg] = pg.op("vector",
                               lambda e, tg=tg, bk=bk, dst=dst: e.tensor_copy(
                                   dst[:, tg * 512:(tg + 1) * 512],
                                   ppb[bk][:, :]),
                               inc=s_dve, waits=[(s_pe, mm_last[tg])])
            else:
                ev[tg] = pg.op("vector",
                               lambda e, tg=tg, bk=bk, dst=dst, scale=scale:
                               e.tensor_scalar(
                                   dst[:, tg * 512:(tg + 1) * 512],
                                   ppb[bk][:, :], scale, None, ALU.mult),
                               inc=s_dve, waits=[(s_pe, mm_last[tg])])
            st["proj_evac"][bk] = ev[tg]
        return mm_last, ev

    def prefetch(wi, w_d, w_sb, gate_tok):
        wtok[h + 1][wi] = pg.op(
            "gpsimd",
            lambda e, w_d=w_d, w_sb=w_sb, h=h: e.dma_start(
                w_sb[:, :],
                w_d.ap()[h + 1].rearrange("p c o -> p (c o)")),
            inc=sem["wpre"][wi], amt=16, waits=[(s_pe, gate_tok)])

    v_mm, v_ev = proj(b["wv"], vTs, 2)
    if h + 1 < NH:
        prefetch(2, dram["wv"], b["wv"], v_mm[3])
    q_mm, q_ev = proj(b["wq"], QT, 0, scale=1.0 / math.sqrt(DH))
    if h + 1 < NH:
        prefetch(0, dram["wq"], b["wq"], q_mm[3])
    k_mm, k_ev = proj(b["wk"], KT, 1)
    if h + 1 < NH:
        prefetch(1, dram["wk"], b["wk"], k_mm[3])

    last_tr = 0
    for g4 in range(4):
        for j in range(4):
            tc = g4 * 4 + j
            w = []
            if j == 0:
                w = [(s_dve, v_ev[g4]), (s_dve, st["ptp_copy"])]
                if g4 == 0:
                    w.append((s_pool, st["vaug_ones"]))
                    w.append((s_pe, st["vaug_reader"]))
            last_tr = pg.op("tensor",
                            lambda e, tc=tc, j=j: e.transpose(
                                ptp[:, j * 128:(j + 1) * 128],
                                vTs[:, tc * 128:(tc + 1) * 128], idb[:, :]),
                            inc=s_pe, waits=w)
        st["ptp_copy"] = pg.op(
            "vector",
            lambda e, g4=g4: e.tensor_copy(
                bass.AP(vaug, (4 * g4) * VSLOT,
                        [[NT * VSLOT, P], [VSLOT, 4], [1, 128]]),
                ptp[:, 0:512]),
            inc=s_dve, waits=[(s_pe, last_tr)])
    st["vts_tok"] = last_tr
    vaug_ready = st["ptp_copy"]

    extok = {}

    pend_z = st["pend_z"]

    def flush_z():
        for th in pend_z:
            th()
        pend_z.clear()

    def emit_av(qt):
        c0 = max(0, qt - dk)
        mm = 0
        for c in range(c0, qt + 1):
            ps_ = off[c] + (qt - c) * 128
            sc_, tok = extok[(c, qt)]
            w = [(sc_, tok)]
            if c == c0:
                w.append((s_dve, st["pz_guard"][qt % 2]))
                w.append((s_dve, vaug_ready))
            mm = pg.op("tensor",
                       lambda e, ps_=ps_, c=c, c0=c0, qt=qt: e.matmul(
                           pz[qt % 2][:, 0:129],
                           probsT[:, ps_: ps_ + 128],
                           vaug[:, c * VSLOT: c * VSLOT + 129],
                           start=(c == c0), stop=(c == qt)),
                       inc=s_pe, waits=w)
        r = pg.op("vector",
                  lambda e, qt=qt: e.reciprocal(zcol[:, 4 + qt % 2:5 + qt % 2],
                                                pz[qt % 2][:, 128:129]),
                  inc=s_dve, waits=[(s_pe, mm)])
        zb = qt % 2
        zn = pg.op("vector",
                   lambda e, zb=zb, qt=qt: e.tensor_scalar(
                       ztm[zb][:, :], pz[qt % 2][:, 0:128],
                       zcol[:, 4 + qt % 2:5 + qt % 2], None, ALU.mult),
                   inc=s_dve,
                   waits=[(s_dve, r), (s_pe, st["ztr_tok_pe"][zb])])
        st["pz_guard"][qt % 2] = zn

        def emit_z(h=h, qt=qt, zb=zb, zn=zn):
            zt = pg.op("tensor",
                       lambda e, zb=zb: e.transpose(
                           ptp[:, 0:128], ztm[zb][:, :], idb[:, :]),
                       inc=s_pe,
                       waits=[(s_dve, zn), (s_dve, st["ptp_copy"])])
            st["ztr_tok_pe"][zb] = zt
            st["ptp_copy"] = pg.op(
                "vector",
                lambda e, h=h, qt=qt: e.tensor_copy(
                    zT[:, h * SEQL + qt * 128: h * SEQL + (qt + 1) * 128],
                    ptp[:, 0:128]),
                inc=s_dve, waits=[(s_pe, zt)])
        flush_z()
        pend_z.append(emit_z)

    rows = []
    for c in range(NT):
        q1 = min(c + dk, NT - 1)
        spans = []
        qq = c
        while qq <= q1:
            if h in GROUP_HEADS:
                ge = min(q1, (qq // 4) * 4 + 3)
            else:
                ge = min(q1, qq + 3)
            spans.append((qq, ge))
            qq = ge + 1
        rows.append(spans)
    si = 0
    first_exp = True
    for c in range(NT):
        for (q0, q1) in rows[c]:
            n = (q1 - q0 + 1) * 128
            bank = si % 3
            buf = si % 3
            si += 1
            mm = pg.op("tensor",
                       lambda e, c=c, q0=q0, n=n, bank=bank: e.matmul(
                           psc[bank][:, 0:n],
                           KT[:, c * 128:(c + 1) * 128],
                           QT[:, q0 * 128: q0 * 128 + n],
                           start=True, stop=True),
                       inc=s_pe,
                       waits=[(s_dve, st["sc_guard"][bank]),
                              (s_dve, k_ev[c // 4]), (s_dve, q_ev[3])])
            ev = pg.op("vector",
                       lambda e, n=n, bank=bank, buf=buf: e.tensor_scalar(
                           spanb[buf][:, 0:n], psc[bank][:, 0:n],
                           1024.0, -1024.0, ALU.min, ALU.max),
                       inc=s_dve,
                       waits=[(s_pe, mm), (s_act, st["span_exp"][buf]),
                              (s_pool, st["span_al"][buf])])
            st["sc_guard"][bank] = ev
            has_diag = (q0 == c)
            dgt = None
            if has_diag:
                dgt = pg.op("gpsimd",
                            lambda e, buf=buf: e.tensor_tensor(
                                diagb[:, :], spanb[buf][:, 0:128],
                                cst[:, CST_CAUSALT:CST_CAUSALT + 128],
                                ALU.add),
                            inc=s_pool,
                            waits=[(s_dve, ev), (s_act, st["diag_exp"])])
                st["span_al"][buf] = dgt
            if h in GROUP_HEADS:
                gd = 4 * (q0 // 4) - c
                pieces = [(q0, (q1 - q0 + 1) * 128, _t2_off[h] + gd + 3, 128)]
            elif h in PERQT_HEADS:
                pieces = [(q, 128, _t1_off[h] + (q - c), 128)
                          for q in range(q0, q1 + 1)]
            else:
                pieces = []
                for q in range(q0, q1 + 1):
                    for u in range(2):
                        pieces.append((q, 64,
                                       _t0_off[h] + 2 * (q - c) + u, 64 * u))
            last_exp = st["span_exp"][buf]
            for (qa, ncols, bcol, uoff) in pieces:
                soff = (qa - q0) * 128 + (uoff if ncols == 64 else 0)
                pslot = off[c] + (qa - c) * 128 + (uoff if ncols == 64 else 0)
                gw = ([(s_pe, st["probsT_guard"])]
                      if first_exp and st["probsT_guard"] else [])
                if has_diag and qa == c:
                    t = pg.op("scalar",
                              lambda e, pslot=pslot, bcol=bcol, ncols=ncols,
                              soff=soff: e.activation(
                                  probsT[:, pslot:pslot + min(ncols, 128)],
                                  diagb[:, soff:soff + min(ncols, 128)]
                                  if ncols == 64 else diagb[:, 0:128],
                                  AF.Exp, bias=cst[:, bcol:bcol + 1]),
                              inc=s_act, waits=[(s_pool, dgt)] + gw)
                    first_exp = False
                    extok[(c, qa)] = (s_act, t)
                    st["diag_exp"] = t
                    last_exp = t
                    if ncols > 128:
                        t = pg.op("scalar",
                                  lambda e, pslot=pslot, soff=soff,
                                  ncols=ncols, buf=buf, bcol=bcol:
                                  e.activation(
                                      probsT[:, pslot + 128: pslot + ncols],
                                      spanb[buf][:, soff + 128: soff + ncols],
                                      AF.Exp, bias=cst[:, bcol:bcol + 1]),
                                  inc=s_act, waits=[(s_dve, ev)])
                        for q in range(qa + 1, q1 + 1):
                            extok[(c, q)] = (s_act, t)
                        last_exp = t
                else:
                    t = pg.op("scalar",
                              lambda e, pslot=pslot, soff=soff, ncols=ncols,
                              buf=buf, bcol=bcol: e.activation(
                                  probsT[:, pslot: pslot + ncols],
                                  spanb[buf][:, soff: soff + ncols],
                                  AF.Exp, bias=cst[:, bcol:bcol + 1]),
                              inc=s_act, waits=[(s_dve, ev)] + gw)
                    first_exp = False
                    qb2 = (q1 if (h in GROUP_HEADS) else qa)
                    for q in range(qa, qb2 + 1):
                        extok[(c, q)] = (s_act, t)
                    last_exp = t
            st["span_exp"][buf] = last_exp
        if c >= 1:
            emit_av(c - 1)
    emit_av(NT - 1)
    if h == NH - 1:
        flush_z()
    st["vaug_reader"] = s_pe.n
    st["probsT_guard"] = s_pe.n


def _plan_ffn(pg, seq, b, sb, sem, wf_tok, lora_tok, x2_d, out_d, prefix):
    s_pe, s_dve, s_act, s_pool = sem["pe"], sem["dve"], sem["act"], sem["pool"]
    s_dx, s_dw, s_dt, s_do = sem["dx"], sem["dw"], sem["dt"], sem["do"]
    s_dl, s_dc = sem["dl"], sem["dc"]
    s_wg = sem["wg"]
    cst, e8, idf, onesb = sb["cst"], sb["e8"], sb["idf"], sb["onesb"]
    idb = sb["idb"]

    apply_tok = [0, 0]
    xnb_rd = [0, 0]
    xnT2_rd = [0, 0]
    ptc_copy = [0, 0]
    ev_out = [0, 0]
    store_tok = [0, 0]
    gl_evac = 0
    gl_user = 0
    wts_evac = 0
    wrep_evac = 0
    xa_evac = 0
    wxa_tok = 0
    base_evac = [0, 0]
    xa_reader = 0
    wxa_reader = 0
    IOTA = cst[:, CST_IOTA8:CST_IOTA8 + 8]

    for i in range(NT):
        bi = i % 2
        if i < len(prefix):
            ld, t_apply = prefix[i]
        else:
            ld = pg.op("scalar",
                       lambda e, i=i, bi=bi: e.dma_start(
                           b["x2b"][bi][:, :], x2_d.ap()[i]),
                       inc=sem["cxl"][bi], amt=16,
                       waits=[(s_pool, apply_tok[bi]), (s_dve, ev_out[bi])])
            t_apply = _plan_ln(pg, b["x2b"][bi], b["xnb"][bi], b["xnb"][bi],
                               b["cc"], 8 * bi, sem, (sem["cxl"][bi], ld),
                               [(s_pe, xnb_rd[bi])], eng="gpsimd")
        apply_tok[bi] = t_apply
        trs = []
        last_tr = 0
        for g4 in range(4):
            pk = g4 % 2
            for j in range(4):
                dc = g4 * 4 + j
                w = []
                if j == 0:
                    w = [(s_pool, t_apply), (s_dve, ptc_copy[pk])]
                last_tr = pg.op("tensor",
                                lambda e, bi=bi, dc=dc, j=j, pk=pk: e.transpose(
                                    b["ptc"][pk][:, j * 128:(j + 1) * 128],
                                    b["xnb"][bi][:, dc * 128:(dc + 1) * 128],
                                    idb[:, :]),
                                inc=s_pe, waits=w)
            ct = pg.op("vector",
                       lambda e, bi=bi, g4=g4, pk=pk: e.tensor_copy(
                           b["xnT2"][bi][:, g4 * 512:(g4 + 1) * 512],
                           b["ptc"][pk][:, 0:512]),
                       inc=s_dve,
                       waits=[(s_pe, last_tr)] +
                             ([(s_pe, xnT2_rd[bi])] if g4 == 0 else []))
            ptc_copy[pk] = ct
            trs.append(ct)
        xnb_rd[bi] = last_tr

        for dc in range(NDC):
            w = [(s_dve, trs[dc // 4])]
            if dc == 0:
                w.append((s_dve, gl_evac))
                if i == 0:
                    w.append(lora_tok[0])
                    w.append(lora_tok[1])
            pg.op("tensor",
                  lambda e, bi=bi, dc=dc: e.matmul(
                      b["pgl"][:, 0:8],
                      b["xnT2"][bi][:, dc * 128:(dc + 1) * 128],
                      b["gwt"][:, dc * 8:(dc + 1) * 8],
                      start=(dc == 0), stop=False),
                  inc=s_pe, waits=w)
        glmm = pg.op("tensor",
                     lambda e: e.matmul(b["pgl"][:, 0:8], onesb[0:1, :],
                                        b["gbr"][0:1, 0:8],
                                        start=False, stop=True),
                     inc=s_pe)
        gl_evac = pg.op("vector",
                        lambda e: e.tensor_copy(b["gl"][:, :], b["pgl"][:, 0:8]),
                        inc=s_dve, waits=[(s_pe, glmm), (s_dve, gl_user)])
        xamm = 0
        for dc in range(NDC):
            w = [(s_dve, trs[dc // 4])]
            if dc == 0:
                w.append((s_dve, xa_evac))
                w.append((s_pe, xa_reader))
                if i == 0:
                    w.append(lora_tok[2])
            xamm = pg.op("tensor",
                         lambda e, bi=bi, dc=dc: e.matmul(
                             b["pxa"][:, :],
                             b["at"][:, dc * 128:(dc + 1) * 128],
                             b["xnT2"][bi][:, dc * 128:(dc + 1) * 128],
                             start=(dc == 0), stop=(dc == NDC - 1)),
                         inc=s_pe, waits=w)
        xa_reader = xamm
        xa_evac = pg.op("vector",
                        lambda e: e.tensor_copy(b["xa"][:, :], b["pxa"][:, :]),
                        inc=s_dve, waits=[(s_pe, xamm), (s_dve, wxa_tok)])

        g = b["gsc"]
        t = pg.op("vector", lambda e: e.max(g[:, 0:8], b["gl"][:, :]),
                  inc=s_dve, waits=[(s_dve, gl_evac)])
        t = pg.op("vector",
                  lambda e: e.tensor_scalar(g[:, 8:16], b["gl"][:, :],
                                            g[:, 0:1], None, ALU.is_equal),
                  inc=s_dve, waits=[(s_dve, t)])
        t = pg.op("vector",
                  lambda e: e.tensor_tensor(g[:, 40:48], g[:, 8:16], IOTA,
                                            ALU.mult),
                  inc=s_dve, waits=[(s_dve, t)])
        t = pg.op("vector",
                  lambda e: e.tensor_scalar(g[:, 48:56], g[:, 8:16],
                                            -1e9, 1e9, ALU.mult, ALU.add),
                  inc=s_dve, waits=[(s_dve, t)])
        t = pg.op("vector",
                  lambda e: e.tensor_tensor(g[:, 24:32], g[:, 40:48],
                                            g[:, 48:56], ALU.add),
                  inc=s_dve, waits=[(s_dve, t)])
        t = pg.op("vector",
                  lambda e: e.tensor_reduce(g[:, 32:33], g[:, 24:32],
                                            AX.X, ALU.min),
                  inc=s_dve, waits=[(s_dve, t)])
        t = pg.op("vector",
                  lambda e: e.tensor_scalar(g[:, 8:16], IOTA,
                                            g[:, 32:33], None, ALU.is_equal),
                  inc=s_dve, waits=[(s_dve, t)])
        t = pg.op("vector",
                  lambda e: e.tensor_scalar(g[:, 16:24], b["gl"][:, :],
                                            g[:, 1:2], None, ALU.is_equal),
                  inc=s_dve, waits=[(s_dve, t)])
        gl_user = t
        t = pg.op("vector",
                  lambda e: e.tensor_scalar(g[:, 24:32], g[:, 8:16],
                                            -1.0, 1.0, ALU.mult, ALU.add),
                  inc=s_dve, waits=[(s_dve, t)])
        t = pg.op("vector",
                  lambda e: e.tensor_tensor(g[:, 40:48], g[:, 16:24],
                                            g[:, 24:32], ALU.mult),
                  inc=s_dve, waits=[(s_dve, t)])
        t = pg.op("vector",
                  lambda e: e.tensor_scalar(g[:, 48:56], g[:, 40:48],
                                            -1e9, 1e9, ALU.mult, ALU.add),
                  inc=s_dve, waits=[(s_dve, t)])
        t = pg.op("vector",
                  lambda e: e.tensor_tensor(g[:, 24:32], g[:, 40:48], IOTA,
                                            ALU.mult),
                  inc=s_dve, waits=[(s_dve, t)])
        t = pg.op("vector",
                  lambda e: e.tensor_tensor(g[:, 24:32], g[:, 24:32],
                                            g[:, 48:56], ALU.add),
                  inc=s_dve, waits=[(s_dve, t)])
        t = pg.op("vector",
                  lambda e: e.tensor_reduce(g[:, 33:34], g[:, 24:32],
                                            AX.X, ALU.min),
                  inc=s_dve, waits=[(s_dve, t)])
        t = pg.op("vector",
                  lambda e: e.tensor_scalar(g[:, 16:24], IOTA,
                                            g[:, 33:34], None, ALU.is_equal),
                  inc=s_dve, waits=[(s_dve, t)])
        t = pg.op("vector",
                  lambda e: e.tensor_tensor(g[:, 34:35], g[:, 1:2],
                                            g[:, 0:1], ALU.subtract),
                  inc=s_dve, waits=[(s_dve, t)])
        te = pg.op("scalar",
                   lambda e: e.activation(g[:, 35:36], g[:, 34:35], AF.Exp),
                   inc=s_act, waits=[(s_dve, t)])
        t = pg.op("vector",
                  lambda e: e.tensor_scalar(g[:, 36:37], g[:, 35:36],
                                            1.0, None, ALU.add),
                  inc=s_dve, waits=[(s_act, te)])
        t = pg.op("vector",
                  lambda e: e.reciprocal(g[:, 37:38], g[:, 36:37]),
                  inc=s_dve, waits=[(s_dve, t)])
        t = pg.op("vector",
                  lambda e: e.tensor_tensor(g[:, 38:39], g[:, 35:36],
                                            g[:, 37:38], ALU.mult),
                  inc=s_dve, waits=[(s_dve, t)])
        t = pg.op("vector",
                  lambda e: e.tensor_scalar(g[:, 40:48], g[:, 8:16],
                                            g[:, 37:38], None, ALU.mult),
                  inc=s_dve, waits=[(s_dve, t)])
        t = pg.op("vector",
                  lambda e: e.tensor_scalar(g[:, 48:56], g[:, 16:24],
                                            g[:, 38:39], None, ALU.mult),
                  inc=s_dve, waits=[(s_dve, t)])
        wv_tok = pg.op("vector",
                       lambda e: e.tensor_tensor(g[:, 16:24], g[:, 40:48],
                                                 g[:, 48:56], ALU.add),
                       inc=s_dve, waits=[(s_dve, t)])
        ev4 = [None] * 4
        mm = 0

        def base_chunks(og):
            bk = og % 2
            m = 0
            for dc in range(NDC):
                w = [(s_dve, trs[dc // 4])]
                if og == 0 and dc % 2 == 0:
                    w.append((s_wg[dc // 2], wf_tok[(dc // 2) * 2 + 1]))
                if dc == 0:
                    w.append((s_dve, base_evac[bk]))
                m = pg.op("tensor",
                          lambda e, bi=bi, dc=dc, og=og, bk=bk: e.matmul(
                              b["pb"][bk][:, :],
                              b["xnT2"][bi][:, dc * 128:(dc + 1) * 128],
                              b["wfT"][:, dc * D + og * 512:
                                       dc * D + (og + 1) * 512],
                              start=(dc == 0), stop=False),
                          inc=s_pe, waits=w)
            return m

        base_chunks(0)
        base_chunks(1)

        twt = pg.op("tensor",
                    lambda e: e.transpose(b["pwt"][:, :], g[:, 16:24],
                                          idf[:, :]),
                    inc=s_pe, waits=[(s_dve, wv_tok), (s_dve, wts_evac)])
        wts_evac = pg.op("vector",
                         lambda e: e.tensor_copy(b["wts"][:, :],
                                                 b["pwt"][:, :]),
                         inc=s_dve, waits=[(s_pe, twt)])
        twr = pg.op("tensor",
                    lambda e: e.matmul(b["pwr"][:, :], e8[:, :],
                                       b["wts"][0:8, :],
                                       start=True, stop=True),
                    inc=s_pe, waits=[(s_dve, wts_evac), (s_dve, wrep_evac)])
        wrep_evac = pg.op("vector",
                          lambda e: e.tensor_copy(b["wrs"][:, :],
                                                  b["pwr"][:, :]),
                          inc=s_dve, waits=[(s_pe, twr)])
        wxa_tok = pg.op("vector",
                        lambda e: e.tensor_tensor(b["wxa"][:, :], b["xa"][:, :],
                                                  b["wrs"][:, :], ALU.mult),
                        inc=s_dve,
                        waits=[(s_dve, wrep_evac), (s_dve, xa_evac),
                               (s_pe, wxa_reader)])

        def finish_og(og, first_delta):
            bk = og % 2
            m = pg.op("tensor",
                      lambda e, og=og, bk=bk: e.matmul(
                          b["pb"][bk][:, :], onesb[0:1, :],
                          b["fbr"][0:1, og * 512:(og + 1) * 512],
                          start=False, stop=False),
                      inc=s_pe,
                      waits=[lora_tok[4]] if (i == 0 and og == 0) else [])
            m = pg.op("tensor",
                      lambda e, og=og, bk=bk: e.matmul(
                          b["pb"][bk][:, :], b["wxa"][:, :],
                          b["bt"][:, og * 512:(og + 1) * 512],
                          start=False, stop=True),
                      inc=s_pe,
                      waits=([(s_dve, wxa_tok)] if first_delta else []) +
                            ([lora_tok[3]] if (i == 0 and og == 0) else []))
            ev = pg.op("vector",
                       lambda e, bi=bi, og=og, bk=bk: e.tensor_tensor(
                           b["outb"][bi][:, og * 512:(og + 1) * 512],
                           b["pb"][bk][:, :],
                           b["x2b"][bi][:, og * 512:(og + 1) * 512], ALU.add),
                       inc=s_dve,
                       waits=[(s_pe, m), (sem["dob"][bi], store_tok[bi])])
            base_evac[bk] = ev
            ev4[og] = ev
            return m

        finish_og(0, True)
        finish_og(1, False)
        base_chunks(2)
        base_chunks(3)
        finish_og(2, False)
        mm = finish_og(3, False)
        wxa_reader = mm
        xnT2_rd[bi] = mm
        ev_out[bi] = ev4[-1]
        store_tok[bi] = pg.op("sync",
                              lambda e, i=i, bi=bi: e.dma_start(
                                  out_d.ap()[seq][i], b["outb"][bi][:, :]),
                              inc=sem["dob"][bi], amt=16,
                              waits=[(s_dve, t_) for t_ in ev4])


def _host_consts():
    cst = np.zeros((P, CST_NCOL), dtype=np.float32)
    p = np.arange(P)
    cst[:, CST_CAUSALT:CST_CAUSALT + 128] = np.where(
        p[:, None] > p[None, :], NEG, 0.0)
    cst[:, CST_CAUSALTM:CST_CAUSALTM + 128] = np.where(
        p[None, :] > p[:, None], NEG, 0.0)
    for h in EXACT_HEADS:
        sl = SLOPES[h]
        for d in range(DK[h] + 1):
            for u in range(2):
                cst[:, _t0_off[h] + 2 * d + u] = sl * (p - 63 - 128 * d - 64 * u)
    for h in PERQT_HEADS:
        sl = SLOPES[h]
        for d in range(DK[h] + 1):
            cst[:, _t1_off[h] + d] = sl * (p - 127 - 128 * d)
    for h in GROUP_HEADS:
        sl = SLOPES[h]
        for gd in range(-3, 13):
            cst[:, _t2_off[h] + gd + 3] = sl * (p - 511 - 128 * gd)
    cst[:, CST_IOTA8:CST_IOTA8 + 8] = np.arange(8)[None, :]
    cst[:, CST_EPS] = 1e-5
    cst[:, CST_MHALF] = -0.5
    return cst


def kernel(L0_attn_resid_pre, L0_attn_Wq, L0_attn_Wk, L0_attn_Wv, L0_attn_Wo,
           L0_ffn_W, L0_ffn_b, L0_ffn_lora_A, L0_ffn_lora_B, L0_ffn_gate_w,
           L0_ffn_gate_b):
    f32 = np.float32
    x = np.ascontiguousarray(np.asarray(L0_attn_resid_pre, dtype=f32))
    Wq = np.asarray(L0_attn_Wq, dtype=f32)
    Wk = np.asarray(L0_attn_Wk, dtype=f32)
    Wv = np.asarray(L0_attn_Wv, dtype=f32)
    Wo = np.asarray(L0_attn_Wo, dtype=f32)
    Wf = np.asarray(L0_ffn_W, dtype=f32)
    fb = np.asarray(L0_ffn_b, dtype=f32)
    la = np.asarray(L0_ffn_lora_A, dtype=f32)
    lb = np.asarray(L0_ffn_lora_B, dtype=f32)
    gw = np.asarray(L0_ffn_gate_w, dtype=f32)
    gb = np.asarray(L0_ffn_gate_b, dtype=f32)

    def tile_qkv(W):
        return np.ascontiguousarray(
            W.reshape(NH, DH, NDC, P).transpose(0, 3, 2, 1))

    common = {
        "wq": tile_qkv(Wq), "wk": tile_qkv(Wk), "wv": tile_qkv(Wv),
        "wo": np.ascontiguousarray(Wo.T.reshape(NDC, P, D)),
        "wf": np.ascontiguousarray(Wf.T.reshape(NDC, P, D)),
        "fb": np.ascontiguousarray(fb.reshape(1, D)),
        "la": np.ascontiguousarray(la.transpose(2, 0, 1).reshape(NDC, P, 128)),
        "lb": np.ascontiguousarray(lb.transpose(0, 2, 1).reshape(128, D)),
        "gw": np.ascontiguousarray(gw.T.reshape(NDC, P, 8)),
        "gb": np.ascontiguousarray(gb.reshape(1, 8)),
        "arep": np.ascontiguousarray(np.broadcast_to(
            (np.arange(512, dtype=f32)[None, :] *
             np.asarray(SLOPES[:2], dtype=f32)[:, None])[:, None, :],
            (2, P, 512))),
        "cst": _host_consts(),
        "e8": np.ascontiguousarray(
            (np.arange(8)[:, None] == (np.arange(128)[None, :] // 16))
            .astype(f32)),
        "idf": np.eye(P, dtype=f32),
    }

    nc = build_program()
    in_maps = []
    for c in range(NCORES):
        m = dict(common)
        m["x"] = np.ascontiguousarray(
            x[2 * c:2 * c + 2].reshape(NSEQ, NT, P, D))
        in_maps.append(m)

    res = run_bass_kernel_spmd(nc, in_maps, core_ids=list(range(NCORES)))
    out = np.concatenate(
        [r["out"].reshape(NSEQ, SEQL, D) for r in res.results], axis=0)
    return out


# version 19
# speedup vs baseline: 1.0526x; 1.0526x over previous
import math
from contextlib import ExitStack

import numpy as np
import ml_dtypes

import concourse.bass as bass
import concourse.mybir as mybir
from concourse.bass_utils import run_bass_kernel_spmd

F32 = mybir.dt.float32
BF16 = mybir.dt.bfloat16
AF = mybir.ActivationFunctionType
ALU = mybir.AluOpType
AX = mybir.AxisListType

P = 128
D = 2048
NH = 16
DH = 128
NDC = 16
NT = 16
SEQL = 2048
NSEQ = 2
NCORES = 8

SLOPES = [2.0 ** (-0.5 * (h + 1)) for h in range(NH)]
DK = [min(15, int(136.0 / (128.0 * s)) + 1) for s in SLOPES]
EXACT_HEADS = (0, 1)
PERQT_HEADS = (2, 3, 4, 5)
GROUP_HEADS = tuple(range(6, 16))
NEG = -1.0e30

def _prows(dk):
    offs = []
    off = 0
    for c in range(NT):
        q1 = min(c + dk, NT - 1)
        offs.append((off, c, q1))
        off += (q1 - c + 1) * P
    return offs, off

PROW_MAX = max(_prows(dk)[1] for dk in DK)

CST_CAUSALT = 0
CST_CAUSALTM = 128
_t0_off = {}
_c = 256
for _h in EXACT_HEADS:
    _t0_off[_h] = _c
    _c += (DK[_h] + 1) * 2
_t1_off = {}
for _h in PERQT_HEADS:
    _t1_off[_h] = _c
    _c += DK[_h] + 1
_t2_off = {}
for _h in GROUP_HEADS:
    _t2_off[_h] = _c
    _c += 16
CST_IOTA8 = _c; _c += 8
CST_EPS = _c; _c += 1
CST_MHALF = _c; _c += 1
CST_NCOL = _c

VSLOT = 144


class Ctr:
    def __init__(self, h):
        self.h = h
        self.n = 0


class Prog:
    ENGS = ("sync", "gpsimd", "vector", "scalar", "tensor")

    def __init__(self, nc):
        self.nc = nc
        self.cur = None

    def new_block(self):
        self.cur = {e: [] for e in self.ENGS}

    def op(self, eng, f, inc=None, amt=1, waits=()):
        tok = None
        if inc is not None:
            inc.n += amt
            tok = inc.n
        wl = [(c.h, v) for (c, v) in waits if v and v > 0]

        def th(e, f=f, wl=wl, inc=inc, amt=amt):
            for (h, v) in wl:
                e.wait_ge(h, v)
            ins = f(e)
            if inc is not None:
                ins.then_inc(inc.h, amt)
        self.cur[eng].append(th)
        return tok

    def wait(self, eng, ctr, val):
        if val and val > 0:
            self.cur[eng].append(lambda e, h=ctr.h, v=val: e.wait_ge(h, v))

    def run_block(self, blk):
        for name in self.ENGS:
            lst = self.cur[name]
            if not lst:
                continue
            getattr(blk, name)(lambda e, lst=lst: [t(e) for t in lst])
        self.cur = None


def build_program(debug=False):
    nc = bass.Bass("TRN2", target_bir_lowering=False)

    x_d = nc.dram_tensor("x", [NSEQ, NT, P, D], F32, kind="ExternalInput")
    wq_d = nc.dram_tensor("wq", [NH, P, NDC, DH], F32, kind="ExternalInput")
    wk_d = nc.dram_tensor("wk", [NH, P, NDC, DH], F32, kind="ExternalInput")
    wv_d = nc.dram_tensor("wv", [NH, P, NDC, DH], F32, kind="ExternalInput")
    wo_d = nc.dram_tensor("wo", [NDC, P, D], F32, kind="ExternalInput")
    wf_d = nc.dram_tensor("wf", [NDC, P, D], F32, kind="ExternalInput")
    fb_d = nc.dram_tensor("fb", [1, D], F32, kind="ExternalInput")
    la_d = nc.dram_tensor("la", [NDC, P, 128], F32, kind="ExternalInput")
    lb_d = nc.dram_tensor("lb", [128, D], F32, kind="ExternalInput")
    gw_d = nc.dram_tensor("gw", [NDC, P, 8], F32, kind="ExternalInput")
    gb_d = nc.dram_tensor("gb", [1, 8], F32, kind="ExternalInput")
    arep_d = nc.dram_tensor("arep", [2, P, 512], F32, kind="ExternalInput")
    cst_d = nc.dram_tensor("cst", [P, CST_NCOL], F32, kind="ExternalInput")
    e8_d = nc.dram_tensor("e8", [8, 128], F32, kind="ExternalInput")
    idf_d = nc.dram_tensor("idf", [P, P], F32, kind="ExternalInput")
    x2_d = nc.dram_tensor("x2d", [NT, P, D], F32, kind="Internal")
    out_d = nc.dram_tensor("out", [NSEQ, NT, P, D], F32, kind="ExternalOutput")
    dbg = None
    if debug:
        dbg = dict(
            xnT=nc.dram_tensor("dxnT", [P, NDC * SEQL], BF16,
                               kind="ExternalOutput"),
            zT=nc.dram_tensor("dzT", [P, NH * SEQL], BF16,
                              kind="ExternalOutput"),
            x2=nc.dram_tensor("dx2", [NT, P, D], F32, kind="ExternalOutput"))

    dram = dict(x=x_d, wq=wq_d, wk=wk_d, wv=wv_d, wo=wo_d, wf=wf_d, fb=fb_d,
                la=la_d, lb=lb_d, gw=gw_d, gb=gb_d, arep=arep_d, cst=cst_d,
                e8=e8_d, idf=idf_d, x2=x2_d, out=out_d)

    es = ExitStack()
    with es:
        sem = dict(
            pe=Ctr(es.enter_context(nc.semaphore("s_pe"))),
            dve=Ctr(es.enter_context(nc.semaphore("s_dve"))),
            act=Ctr(es.enter_context(nc.semaphore("s_act"))),
            pool=Ctr(es.enter_context(nc.semaphore("s_pool"))),
            dx=Ctr(es.enter_context(nc.semaphore("s_dx"))),
            dw=Ctr(es.enter_context(nc.semaphore("s_dw"))),
            dt=Ctr(es.enter_context(nc.semaphore("s_dt"))),
            do=Ctr(es.enter_context(nc.semaphore("s_do"))),
            dl=Ctr(es.enter_context(nc.semaphore("s_dl"))),
            ds=Ctr(es.enter_context(nc.semaphore("s_ds"))),
            dc=Ctr(es.enter_context(nc.semaphore("s_dc"))),
            wg=[Ctr(es.enter_context(nc.semaphore(f"s_wg{i}")))
                for i in range(8)],
            dls=[Ctr(es.enter_context(nc.semaphore(f"s_dls{i}")))
                 for i in range(5)],
            dxa=[Ctr(es.enter_context(nc.semaphore(f"s_dxa{i}")))
                 for i in range(4)],
            bxl=[Ctr(es.enter_context(nc.semaphore(f"s_bxl{i}")))
                 for i in range(2)],
            cxl=[Ctr(es.enter_context(nc.semaphore(f"s_cxl{i}")))
                 for i in range(2)],
            wpre=[Ctr(es.enter_context(nc.semaphore(f"s_wpre{i}")))
                  for i in range(3)],
            dob=[Ctr(es.enter_context(nc.semaphore(f"s_dob{i}")))
                 for i in range(2)],
            dsb=[Ctr(es.enter_context(nc.semaphore(f"s_dsb{i}")))
                 for i in range(2)],
        )
        pg = Prog(nc)

        cst = es.enter_context(nc.sbuf_tensor("cstt", [P, CST_NCOL], F32))
        e8 = es.enter_context(nc.sbuf_tensor("e8t", [8, 128], F32))
        idf = es.enter_context(nc.sbuf_tensor("idft", [P, P], F32))
        idb = es.enter_context(nc.sbuf_tensor("idbt", [P, P], BF16))
        onesb = es.enter_context(nc.sbuf_tensor("onesbt", [1, P], BF16))
        sb = dict(cst=cst, e8=e8, idf=idf, idb=idb, onesb=onesb)

        nc.const_aps.register(F32, 1e-5, cst[:, CST_EPS:CST_EPS + 1])
        nc.const_aps.register(F32, -0.5, cst[:, CST_MHALF:CST_MHALF + 1])

        pg.new_block()
        pg.op("sync", lambda e: e.dma_start(cst[:, :], cst_d.ap()),
              inc=sem["dx"], amt=16)
        pg.op("sync", lambda e: e.dma_start(e8[:, :], e8_d.ap()),
              inc=sem["dx"], amt=16)
        pg.op("sync", lambda e: e.dma_start(idf[:, :], idf_d.ap()),
              inc=sem["dx"], amt=16)
        pg.op("gpsimd", lambda e: e.dma_start(idb[:, :], idf_d.ap()),
              inc=sem["dw"], amt=16)
        pg.op("gpsimd", lambda e: e.memset(onesb[:, :], 1.0), inc=sem["pool"])
        pg.wait("sync", sem["dx"], sem["dx"].n)
        pg.wait("gpsimd", sem["dw"], sem["dw"].n)
        with nc.Block() as blk:
            pg.run_block(blk)

        zT = es.enter_context(nc.sbuf_tensor("zTt", [P, NH * SEQL], BF16))
        sb["zT"] = zT

        dram["dbg"] = dbg
        for s in range(NSEQ):
            _build_seq(nc, pg, s, dram, sb, sem)

        pg.new_block()
        for _p in range(2):
            pg.wait("sync", sem["dob"][_p], sem["dob"][_p].n)
        with nc.Block() as blk:
            pg.run_block(blk)

    return nc


def _build_seq(nc, pg, s, dram, sb, sem):
    s_pe, s_dve, s_act, s_pool = sem["pe"], sem["dve"], sem["act"], sem["pool"]
    s_dx, s_dw, s_dt, s_do = sem["dx"], sem["dw"], sem["dt"], sem["do"]
    s_dl, s_ds, s_dc = sem["dl"], sem["ds"], sem["dc"]
    x_d, out_d, x2_d = dram["x"], dram["out"], dram["x2"]
    cst, idb, zT = sb["cst"], sb["idb"], sb["zT"]

    with ExitStack() as seq_es:
        attn_es = ExitStack()
        xnT = attn_es.enter_context(
            nc.sbuf_tensor(f"xnT{s}", [P, NDC * SEQL], BF16))
        wq_sb = attn_es.enter_context(
            nc.sbuf_tensor(f"wq{s}", [P, NDC * DH], BF16))
        wk_sb = attn_es.enter_context(
            nc.sbuf_tensor(f"wk{s}", [P, NDC * DH], BF16))
        wv_sb = attn_es.enter_context(
            nc.sbuf_tensor(f"wv{s}", [P, NDC * DH], BF16))
        wtok = [[0, 0, 0] for _ in range(NH + 1)]

        with ExitStack() as a2es:
            bufs = dict(
                probsT=a2es.enter_context(
                    nc.sbuf_tensor(f"prT{s}", [P, PROW_MAX], BF16)),
                QT=a2es.enter_context(nc.sbuf_tensor(f"QT{s}", [P, SEQL], BF16)),
                KT=a2es.enter_context(nc.sbuf_tensor(f"KT{s}", [P, SEQL], BF16)),
                vTs=a2es.enter_context(nc.sbuf_tensor(f"vT{s}", [P, SEQL], BF16)),
                vaug=a2es.enter_context(
                    nc.sbuf_tensor(f"va{s}", [P, NT * VSLOT], BF16)),
                spanb=[a2es.enter_context(
                    nc.sbuf_tensor(f"sp{s}{i}", [P, 512], F32))
                    for i in range(3)],
                diagb=a2es.enter_context(
                    nc.sbuf_tensor(f"dg{s}", [P, 128], F32)),
                zcol=a2es.enter_context(nc.sbuf_tensor(f"zc{s}", [P, 8], F32)),
                ztm=[a2es.enter_context(
                    nc.sbuf_tensor(f"zt{s}{i}", [P, 128], BF16))
                    for i in range(2)],
                ppb=[a2es.enter_context(
                    nc.psum_tensor(f"pp{s}{i}", [P, 512], F32))
                    for i in range(2)],
                psc=[a2es.enter_context(
                    nc.psum_tensor(f"ps{s}{i}", [P, 512], F32))
                    for i in range(3)],
                pz=[a2es.enter_context(
                    nc.psum_tensor(f"pz{s}{i}", [P, 129], F32))
                    for i in range(2)],
                ptp=a2es.enter_context(
                    nc.psum_tensor(f"ptp{s}", [P, 512], BF16)),
                xnT=xnT, zT=zT, cst=cst, wq=wq_sb, wk=wk_sb, wv=wv_sb,
                idb=idb)

            pg.new_block()
            pg.op("gpsimd", lambda e: e.memset(
                bass.AP(bufs["vaug"], 128,
                        [[NT * VSLOT, P], [VSLOT, NT], [1, 1]]), 1.0),
                inc=s_pool)

            st = dict(proj_evac=[0, 0], span_al=[0, 0, 0],
                      span_exp=[0, 0, 0], pz_guard=[0, 0],
                      sc_guard=[0, 0, 0], vaug_ones=s_pool.n, vaug_reader=0,
                      vts_tok=0, arep_tok=0, arep_user=0,
                      diag_exp=0, diag_dve=0, span_dve=[0, 0, 0],
                      ptb_guard=[0, 0], probsT_guard=0, zc3_user=0,
                      ptp_copy=0, ztr_tok_pe=[0, 0], pend_z=[])

            pf32 = bufs["probsT"].bitcast(F32)
            xbuf = [pf32[:, 0:2048], pf32[:, 2048:4096]]
            cc = pf32[:, 4096:4112]
            xnb = [bufs["probsT"][:, 12288:14336],
                   bufs["probsT"][:, 14336:16384]]
            for w_d, w_sb in ((dram["wv"], wv_sb), (dram["wq"], wq_sb),
                              (dram["wk"], wk_sb)):
                wi = {id(wq_sb): 0, id(wk_sb): 1, id(wv_sb): 2}[id(w_sb)]
                wtok[0][wi] = pg.op(
                    "gpsimd",
                    lambda e, w_d=w_d, w_sb=w_sb: e.dma_start(
                        w_sb[:, :],
                        w_d.ap()[0].rearrange("p c o -> p (c o)")),
                    inc=sem["wpre"][wi], amt=16)
            apply_tok = [0, 0]
            xnb_guard = [0, 0]
            cp_tok = [[0] * 4 for _ in range(NT)]
            for i in range(NT):
                bb = i % 2
                ld = pg.op("sync",
                           lambda e, i=i, bb=bb: e.dma_start(
                               xbuf[bb], x_d.ap()[s][i]),
                           inc=sem["dxa"][bb], amt=16,
                           waits=[(s_pool, apply_tok[bb])])
                t8 = _plan_ln(pg, xbuf[bb], xnb[bb], xnb[bb], cc, 8 * bb,
                              sem, (sem["dxa"][bb], ld),
                              [(s_pe, xnb_guard[bb])], eng="gpsimd")
                apply_tok[bb] = t8
                last_t = 0
                for g in range(4):
                    for j in range(4):
                        dc = g * 4 + j
                        w = []
                        if j == 0:
                            w = [(s_pool, t8), (s_dve, st["ptp_copy"])]
                        tk = pg.op(
                            "tensor",
                            lambda e, bb=bb, dc=dc, j=j: e.transpose(
                                bufs["ptp"][:, j * 128:(j + 1) * 128],
                                xnb[bb][:, dc * 128:(dc + 1) * 128],
                                sb["idb"][:, :]),
                            inc=(s_pe if j == 3 else None), waits=w)
                        if j == 3:
                            last_t = tk
                    ct = pg.op("vector",
                               lambda e, i=i, g=g: e.tensor_copy(
                                   bass.AP(xnT, (4 * g) * SEQL + i * 128,
                                           [[NDC * SEQL, P], [SEQL, 4],
                                            [1, 128]]),
                                   bufs["ptp"][:, 0:512]),
                               inc=s_dve, waits=[(s_pe, last_t)])
                    st["ptp_copy"] = ct
                    cp_tok[i][g] = ct
                xnb_guard[bb] = last_t
            st["xnT_cp"] = cp_tok

            for h in range(NH):
                _plan_head(pg, s, h, bufs, dram, sem, st, wtok)

            pg.wait("sync", s_dt, s_dt.n)
            pg.wait("sync", s_dx, s_dx.n)
            for _w in range(3):
                pg.wait("gpsimd", sem["wpre"][_w], sem["wpre"][_w].n)
            with nc.Block() as blk:
                pg.run_block(blk)

            if s == 0 and dram.get("dbg"):
                pg.new_block()
                pg.op("sync", lambda e: e.dma_start(
                    dram["dbg"]["zT"].ap(), zT[:, :]), inc=s_dx, amt=16)
                pg.wait("sync", s_dx, s_dx.n)
                with nc.Block() as blk:
                    pg.run_block(blk)

        attn_es.close()
        bces = ExitStack()
        cx2b = [bces.enter_context(nc.sbuf_tensor(f"c2{s}{i}", [P, D], F32))
                for i in range(2)]
        cxnb = [bces.enter_context(
            nc.sbuf_tensor(f"cn{s}{i}", [P, D], BF16)) for i in range(2)]
        ccc = bces.enter_context(nc.sbuf_tensor(f"cc{s}", [P, 16], F32))
        with ExitStack() as bes:

            woT = bes.enter_context(
                nc.sbuf_tensor(f"wo{s}", [P, NDC * D], BF16))
            xb = [bes.enter_context(nc.sbuf_tensor(f"bx{s}{i}", [P, D], F32))
                  for i in range(2)]
            x2b = [bes.enter_context(nc.sbuf_tensor(f"b2{s}{i}", [P, D], F32))
                   for i in range(2)]
            po = [bes.enter_context(nc.psum_tensor(f"bp{s}{i}", [P, 512], F32))
                  for i in range(8)]

            pg.new_block()
            s_wg = sem["wg"]
            wo_tok = []
            for c in range(NDC):
                t = pg.op("gpsimd",
                          lambda e, c=c: e.dma_start(
                              woT[:, c * D:(c + 1) * D], dram["wo"].ap()[c]),
                          inc=s_wg[c // 2], amt=16)
                wo_tok.append(t)
            evac_tok = [0, 0]
            store_tok = [0, 0]
            prefix = []
            for i in range(NT):
                b = i % 2
                ld = pg.op("sync",
                           lambda e, i=i, b=b: e.dma_start(
                               xb[b][:, :], x_d.ap()[s][i]),
                           inc=sem["bxl"][b], amt=16,
                           waits=[(s_dve, evac_tok[b])])
                mm_tok = [0] * 4
                for c in range(NDC):
                    for og in range(4):
                        w = []
                        if og == 0 and c % 2 == 0:
                            w.append((s_wg[c // 2], wo_tok[(c // 2) * 2 + 1]))
                        if c == 0:
                            w.append((s_dve, evac_tok[b]))
                        tk = pg.op(
                            "tensor",
                            lambda e, i=i, b=b, c=c, og=og: e.matmul(
                                po[4 * b + og][:, :],
                                zT[:, c * SEQL + i * 128:
                                   c * SEQL + (i + 1) * 128],
                                woT[:, c * D + og * 512:
                                    c * D + (og + 1) * 512],
                                start=(c == 0), stop=(c == NDC - 1)),
                            inc=(s_pe if c == NDC - 1 else None), waits=w)
                        if c == NDC - 1:
                            mm_tok[og] = tk
                etoks = []
                for og in range(4):
                    t = pg.op("vector",
                              lambda e, b=b, og=og: e.tensor_tensor(
                                  x2b[b][:, og * 512:(og + 1) * 512],
                                  po[4 * b + og][:, :],
                                  xb[b][:, og * 512:(og + 1) * 512], ALU.add),
                              inc=s_dve,
                              waits=[(s_pe, mm_tok[og]), (sem["bxl"][b], ld),
                                     (sem["dsb"][b], store_tok[b])])
                    etoks.append(t)
                evac_tok[b] = etoks[-1]
                store_tok[b] = pg.op("gpsimd",
                                     lambda e, i=i, b=b: e.dma_start(
                                         x2_d.ap()[i], x2b[b][:, :]),
                                     inc=sem["dsb"][b], amt=16,
                                     waits=[(s_dve, t) for t in etoks])
                if s == 0 and dram.get("dbg"):
                    pg.op("sync",
                          lambda e, i=i, b=b: e.dma_start(
                              dram["dbg"]["x2"].ap()[i], x2b[b][:, :]),
                          inc=s_dx, amt=16,
                          waits=[(s_dve, t) for t in etoks])
                if i == 1:
                    for t0 in range(2):
                        pld = pg.op("scalar",
                                    lambda e, t0=t0: e.dma_start(
                                        cx2b[t0][:, :], x2_d.ap()[t0]),
                                    inc=sem["cxl"][t0], amt=16,
                                    waits=[(sem["dsb"][t0], store_tok[t0])])
                        pta = _plan_ln(pg, cx2b[t0], cxnb[t0], cxnb[t0],
                                       ccc, 8 * t0, sem,
                                       (sem["cxl"][t0], pld),
                                       [], eng="gpsimd")
                        prefix.append((pld, pta))
            for _p in range(2):
                pg.wait("sync", sem["bxl"][_p], sem["bxl"][_p].n)
                pg.wait("gpsimd", sem["dsb"][_p], sem["dsb"][_p].n)
            for _g in range(8):
                pg.wait("gpsimd", s_wg[_g], s_wg[_g].n)
            with nc.Block() as blk:
                pg.run_block(blk)


        with ExitStack() as ces:
            cb = dict(
                wfT=ces.enter_context(
                    nc.sbuf_tensor(f"wf{s}", [P, NDC * D], BF16)),
                at=ces.enter_context(
                    nc.sbuf_tensor(f"at{s}", [P, NDC * 128], BF16)),
                bt=ces.enter_context(nc.sbuf_tensor(f"bt{s}", [P, D], BF16)),
                gwt=ces.enter_context(
                    nc.sbuf_tensor(f"gw{s}", [P, NDC * 8], BF16)),
                fbr=ces.enter_context(nc.sbuf_tensor(f"fb{s}", [1, D], BF16)),
                gbr=ces.enter_context(nc.sbuf_tensor(f"gb{s}", [1, 8], BF16)),
                x2b=cx2b,
                outb=[ces.enter_context(
                    nc.sbuf_tensor(f"co{s}{i}", [P, D], F32))
                    for i in range(2)],
                xnb=cxnb,
                xnT2=[ces.enter_context(
                    nc.sbuf_tensor(f"cT{s}{i}", [P, NDC * 128], BF16))
                    for i in range(2)],
                cc=ccc,
                gl=ces.enter_context(nc.sbuf_tensor(f"cg{s}", [P, 8], F32)),
                gsc=ces.enter_context(nc.sbuf_tensor(f"cq{s}", [P, 56], F32)),
                wts=ces.enter_context(nc.sbuf_tensor(f"cw{s}", [8, 128], F32)),
                xa=ces.enter_context(nc.sbuf_tensor(f"cx{s}", [P, 128], F32)),
                wrs=ces.enter_context(nc.sbuf_tensor(f"cr{s}", [P, 128], F32)),
                wxa=ces.enter_context(nc.sbuf_tensor(f"cm{s}", [P, 128], BF16)),
                pb=[ces.enter_context(
                    nc.psum_tensor(f"cb{s}{i}", [P, 512], F32))
                    for i in range(2)],
                pxa=ces.enter_context(nc.psum_tensor(f"cpx{s}", [P, 128], F32)),
                pgl=ces.enter_context(nc.psum_tensor(f"cpg{s}", [P, 8], F32)),
                pwt=ces.enter_context(nc.psum_tensor(f"cpw{s}", [8, 128], F32)),
                pwr=ces.enter_context(nc.psum_tensor(f"cpr{s}", [P, 128], F32)),
                ptc=[ces.enter_context(
                    nc.psum_tensor(f"ptc{s}{i}", [P, 512], BF16))
                    for i in range(2)],
            )
            pg.new_block()
            lora_tok = []
            for li, (src, dst) in enumerate((
                    (dram["gw"].ap().rearrange("c p g -> p c g"),
                     cb["gwt"][:, :].rearrange("p (c g) -> p c g", g=8)),
                    (dram["gb"].ap(), cb["gbr"][:, :]),
                    (dram["la"].ap().rearrange("c p r -> p c r"),
                     cb["at"][:, :].rearrange("p (c r) -> p c r", r=128)),
                    (dram["lb"].ap(), cb["bt"][:, :]),
                    (dram["fb"].ap(), cb["fbr"][:, :]))):
                t = pg.op("gpsimd",
                          lambda e, src=src, dst=dst: e.dma_start(dst, src),
                          inc=sem["dls"][li], amt=16)
                lora_tok.append((sem["dls"][li], t))
            wf_tok = []
            for c in range(NDC):
                t = pg.op("gpsimd",
                          lambda e, c=c: e.dma_start(
                              cb["wfT"][:, c * D:(c + 1) * D],
                              dram["wf"].ap()[c]),
                          inc=sem["wg"][c // 2], amt=16)
                wf_tok.append(t)

            _plan_ffn(pg, s, cb, sb, sem, wf_tok, lora_tok, x2_d, out_d,
                      prefix)

            for _p in range(2):
                pg.wait("scalar", sem["cxl"][_p], sem["cxl"][_p].n)
                pg.wait("sync", sem["dob"][_p], sem["dob"][_p].n)
            for _g in range(8):
                pg.wait("gpsimd", sem["wg"][_g], sem["wg"][_g].n)
            for _l in range(5):
                pg.wait("gpsimd", sem["dls"][_l], sem["dls"][_l].n)
            with nc.Block() as blk:
                pg.run_block(blk)
        bces.close()


def _plan_ln(pg, xin, xout_bf, sqb, cc, c0, sem, ld, apply_extra,
             eng="vector"):
    s_dve = sem["dve"] if eng == "vector" else sem["pool"]
    s_act = sem["act"]
    ldc, ldt = ld
    t_sq = pg.op("scalar",
                 lambda e: e.activation(sqb[:, :], xin[:, :], AF.Square,
                                        accum_out=cc[:, c0:c0 + 1]),
                 inc=s_act, waits=[(ldc, ldt)] + list(apply_extra))
    t_sum = pg.op("scalar",
                  lambda e: e.activation(sqb[:, :], xin[:, :], AF.Identity,
                                         accum_out=cc[:, c0 + 1:c0 + 2]),
                  inc=s_act, waits=[(ldc, ldt), (s_act, t_sq)])
    t1 = pg.op(eng,
               lambda e: e.tensor_scalar(cc[:, c0 + 2:c0 + 3],
                                         cc[:, c0 + 1:c0 + 2],
                                         1.0 / D, None, ALU.mult),
               inc=s_dve, waits=[(s_act, t_sum)])
    t2 = pg.op(eng,
               lambda e: e.tensor_scalar(cc[:, c0 + 3:c0 + 4],
                                         cc[:, c0:c0 + 1],
                                         1.0 / D, None, ALU.mult),
               inc=s_dve, waits=[(s_act, t_sq), (s_dve, t1)])
    t3 = pg.op(eng,
               lambda e: e.tensor_tensor(cc[:, c0 + 4:c0 + 5],
                                         cc[:, c0 + 2:c0 + 3],
                                         cc[:, c0 + 2:c0 + 3], ALU.mult),
               inc=s_dve, waits=[(s_dve, t2)])
    t4 = pg.op(eng,
               lambda e: e.tensor_tensor(cc[:, c0 + 5:c0 + 6],
                                         cc[:, c0 + 3:c0 + 4],
                                         cc[:, c0 + 4:c0 + 5], ALU.subtract),
               inc=s_dve, waits=[(s_dve, t3)])
    t5 = pg.op("scalar",
               lambda e: e.activation(cc[:, c0 + 4:c0 + 5],
                                      cc[:, c0 + 5:c0 + 6], AF.Ln, bias=1e-5),
               inc=s_act, waits=[(s_dve, t4)])
    t6 = pg.op("scalar",
               lambda e: e.activation(cc[:, c0 + 6:c0 + 7],
                                      cc[:, c0 + 4:c0 + 5], AF.Exp, scale=-0.5),
               inc=s_act, waits=[(s_act, t5)])
    t7 = pg.op(eng,
               lambda e: e.tensor_scalar(cc[:, c0 + 7:c0 + 8],
                                         cc[:, c0 + 2:c0 + 3],
                                         cc[:, c0 + 6:c0 + 7], -1.0,
                                         ALU.mult, ALU.mult),
               inc=s_dve, waits=[(s_act, t6)])
    t8 = pg.op(eng,
               lambda e: e.tensor_scalar(xout_bf[:, :], xin[:, :],
                                         cc[:, c0 + 6:c0 + 7],
                                         cc[:, c0 + 7:c0 + 8],
                                         ALU.mult, ALU.add),
               inc=s_dve,
               waits=[(s_dve, t7)] + list(apply_extra))
    return t8


def _plan_head(pg, s, h, b, dram, sem, st, wtok):
    s_pe, s_dve, s_act, s_pool = sem["pe"], sem["dve"], sem["act"], sem["pool"]
    s_dx, s_dw, s_dt = sem["dx"], sem["dw"], sem["dt"]
    xnT, zT, cst = b["xnT"], b["zT"], b["cst"]
    probsT, QT, KT, vTs, vaug = (b["probsT"], b["QT"], b["KT"], b["vTs"],
                                 b["vaug"])
    spanb, diagb = b["spanb"], b["diagb"]
    zcol, ztm = b["zcol"], b["ztm"]
    ppb, psc, pz, ptp, idb = b["ppb"], b["psc"], b["pz"], b["ptp"], b["idb"]
    dk = DK[h]
    prow, _ = _prows(dk)
    off = {c: prow[c][0] for c in range(NT)}

    def proj(w_sb, dst, wi, scale=None):
        mm_last = [0] * 4
        ev = [0] * 4
        for tg in range(4):
            bk = tg % 2
            for dc in range(NDC):
                w = []
                if h == 0:
                    w.append((s_dve, st["xnT_cp"][4 * tg + 3][dc // 4]))
                if dc == 0:
                    if tg == 0:
                        w.append((sem["wpre"][wi], wtok[h][wi]))
                    w.append((s_dve, st["proj_evac"][bk]))
                tk = pg.op(
                    "tensor",
                    lambda e, dc=dc, tg=tg, bk=bk, w_sb=w_sb: e.matmul(
                        ppb[bk][:, :],
                        w_sb[:, dc * DH:(dc + 1) * DH],
                        xnT[:, dc * SEQL + tg * 512:
                            dc * SEQL + (tg + 1) * 512],
                        start=(dc == 0), stop=(dc == NDC - 1)),
                    inc=(s_pe if dc == NDC - 1 else None), waits=w)
                if dc == NDC - 1:
                    mm_last[tg] = tk
            bk = tg % 2
            if scale is None:
                ev[tg] = pg.op("vector",
                               lambda e, tg=tg, bk=bk, dst=dst: e.tensor_copy(
                                   dst[:, tg * 512:(tg + 1) * 512],
                                   ppb[bk][:, :]),
                               inc=s_dve, waits=[(s_pe, mm_last[tg])])
            else:
                ev[tg] = pg.op("vector",
                               lambda e, tg=tg, bk=bk, dst=dst, scale=scale:
                               e.tensor_scalar(
                                   dst[:, tg * 512:(tg + 1) * 512],
                                   ppb[bk][:, :], scale, None, ALU.mult),
                               inc=s_dve, waits=[(s_pe, mm_last[tg])])
            st["proj_evac"][bk] = ev[tg]
        return mm_last, ev

    def prefetch(wi, w_d, w_sb, gate_tok):
        wtok[h + 1][wi] = pg.op(
            "gpsimd",
            lambda e, w_d=w_d, w_sb=w_sb, h=h: e.dma_start(
                w_sb[:, :],
                w_d.ap()[h + 1].rearrange("p c o -> p (c o)")),
            inc=sem["wpre"][wi], amt=16, waits=[(s_pe, gate_tok)])

    v_mm, v_ev = proj(b["wv"], vTs, 2)
    if h + 1 < NH:
        prefetch(2, dram["wv"], b["wv"], v_mm[3])
    q_mm, q_ev = proj(b["wq"], QT, 0, scale=1.0 / math.sqrt(DH))
    if h + 1 < NH:
        prefetch(0, dram["wq"], b["wq"], q_mm[3])
    k_mm, k_ev = proj(b["wk"], KT, 1)
    if h + 1 < NH:
        prefetch(1, dram["wk"], b["wk"], k_mm[3])

    last_tr = 0
    for g4 in range(4):
        for j in range(4):
            tc = g4 * 4 + j
            w = []
            if j == 0:
                w = [(s_dve, v_ev[g4]), (s_dve, st["ptp_copy"])]
                if g4 == 0:
                    w.append((s_pool, st["vaug_ones"]))
                    w.append((s_pe, st["vaug_reader"]))
            tk = pg.op("tensor",
                       lambda e, tc=tc, j=j: e.transpose(
                           ptp[:, j * 128:(j + 1) * 128],
                           vTs[:, tc * 128:(tc + 1) * 128], idb[:, :]),
                       inc=(s_pe if j == 3 else None), waits=w)
            if j == 3:
                last_tr = tk
        st["ptp_copy"] = pg.op(
            "vector",
            lambda e, g4=g4: e.tensor_copy(
                bass.AP(vaug, (4 * g4) * VSLOT,
                        [[NT * VSLOT, P], [VSLOT, 4], [1, 128]]),
                ptp[:, 0:512]),
            inc=s_dve, waits=[(s_pe, last_tr)])
    st["vts_tok"] = last_tr
    vaug_ready = st["ptp_copy"]

    extok = {}

    pend_z = st["pend_z"]

    def flush_z():
        for th in pend_z:
            th()
        pend_z.clear()

    def emit_av(qt):
        c0 = max(0, qt - dk)
        mm = 0
        need = {}
        for c in range(c0, qt + 1):
            sc_, tok = extok[(c, qt)]
            need[id(sc_)] = (sc_, max(tok, need.get(id(sc_), (0, 0))[1]))
        first_w = [(sc_, tok) for (sc_, tok) in need.values()]
        first_w.append((s_dve, st["pz_guard"][qt % 2]))
        first_w.append((s_dve, vaug_ready))
        for c in range(c0, qt + 1):
            ps_ = off[c] + (qt - c) * 128
            w = first_w if c == c0 else []
            tk = pg.op("tensor",
                       lambda e, ps_=ps_, c=c, c0=c0, qt=qt: e.matmul(
                           pz[qt % 2][:, 0:129],
                           probsT[:, ps_: ps_ + 128],
                           vaug[:, c * VSLOT: c * VSLOT + 129],
                           start=(c == c0), stop=(c == qt)),
                       inc=(s_pe if c == qt else None), waits=w)
            if c == qt:
                mm = tk
        r = pg.op("vector",
                  lambda e, qt=qt: e.reciprocal(zcol[:, 4 + qt % 2:5 + qt % 2],
                                                pz[qt % 2][:, 128:129]),
                  inc=s_dve, waits=[(s_pe, mm)])
        zb = qt % 2
        zn = pg.op("vector",
                   lambda e, zb=zb, qt=qt: e.tensor_scalar(
                       ztm[zb][:, :], pz[qt % 2][:, 0:128],
                       zcol[:, 4 + qt % 2:5 + qt % 2], None, ALU.mult),
                   inc=s_dve,
                   waits=[(s_dve, r), (s_pe, st["ztr_tok_pe"][zb])])
        st["pz_guard"][qt % 2] = zn

        def emit_z(h=h, qt=qt, zb=zb, zn=zn):
            zt = pg.op("tensor",
                       lambda e, zb=zb: e.transpose(
                           ptp[:, 0:128], ztm[zb][:, :], idb[:, :]),
                       inc=s_pe,
                       waits=[(s_dve, zn), (s_dve, st["ptp_copy"])])
            st["ztr_tok_pe"][zb] = zt
            st["ptp_copy"] = pg.op(
                "vector",
                lambda e, h=h, qt=qt: e.tensor_copy(
                    zT[:, h * SEQL + qt * 128: h * SEQL + (qt + 1) * 128],
                    ptp[:, 0:128]),
                inc=s_dve, waits=[(s_pe, zt)])
        flush_z()
        pend_z.append(emit_z)

    rows = []
    for c in range(NT):
        q1 = min(c + dk, NT - 1)
        spans = []
        qq = c
        while qq <= q1:
            if h in GROUP_HEADS:
                ge = min(q1, (qq // 4) * 4 + 3)
            else:
                ge = min(q1, qq + 3)
            spans.append((qq, ge))
            qq = ge + 1
        rows.append(spans)
    si = 0
    first_exp = True
    for c in range(NT):
        for (q0, q1) in rows[c]:
            n = (q1 - q0 + 1) * 128
            bank = si % 3
            buf = si % 3
            si += 1
            w_mm = [(s_dve, st["sc_guard"][bank])]
            if q0 == c:
                w_mm += [(s_dve, k_ev[c // 4]), (s_dve, q_ev[3])]
            mm = pg.op("tensor",
                       lambda e, c=c, q0=q0, n=n, bank=bank: e.matmul(
                           psc[bank][:, 0:n],
                           KT[:, c * 128:(c + 1) * 128],
                           QT[:, q0 * 128: q0 * 128 + n],
                           start=True, stop=True),
                       inc=s_pe, waits=w_mm)
            ev = pg.op("vector",
                       lambda e, n=n, bank=bank, buf=buf: e.tensor_scalar(
                           spanb[buf][:, 0:n], psc[bank][:, 0:n],
                           1024.0, -1024.0, ALU.min, ALU.max),
                       inc=s_dve,
                       waits=[(s_pe, mm), (s_act, st["span_exp"][buf]),
                              (s_pool, st["span_al"][buf])])
            st["sc_guard"][bank] = ev
            has_diag = (q0 == c)
            dgt = None
            if has_diag:
                dgt = pg.op("gpsimd",
                            lambda e, buf=buf: e.tensor_tensor(
                                diagb[:, :], spanb[buf][:, 0:128],
                                cst[:, CST_CAUSALT:CST_CAUSALT + 128],
                                ALU.add),
                            inc=s_pool,
                            waits=[(s_dve, ev), (s_act, st["diag_exp"])])
                st["span_al"][buf] = dgt
            if h in GROUP_HEADS:
                gd = 4 * (q0 // 4) - c
                pieces = [(q0, (q1 - q0 + 1) * 128, _t2_off[h] + gd + 3, 128)]
            elif h in PERQT_HEADS:
                pieces = [(q, 128, _t1_off[h] + (q - c), 128)
                          for q in range(q0, q1 + 1)]
            else:
                pieces = []
                for q in range(q0, q1 + 1):
                    for u in range(2):
                        pieces.append((q, 64,
                                       _t0_off[h] + 2 * (q - c) + u, 64 * u))
            last_exp = st["span_exp"][buf]
            for (qa, ncols, bcol, uoff) in pieces:
                soff = (qa - q0) * 128 + (uoff if ncols == 64 else 0)
                pslot = off[c] + (qa - c) * 128 + (uoff if ncols == 64 else 0)
                gw = ([(s_pe, st["probsT_guard"])]
                      if first_exp and st["probsT_guard"] else [])
                if has_diag and qa == c:
                    t = pg.op("scalar",
                              lambda e, pslot=pslot, bcol=bcol, ncols=ncols,
                              soff=soff: e.activation(
                                  probsT[:, pslot:pslot + min(ncols, 128)],
                                  diagb[:, soff:soff + min(ncols, 128)]
                                  if ncols == 64 else diagb[:, 0:128],
                                  AF.Exp, bias=cst[:, bcol:bcol + 1]),
                              inc=s_act, waits=[(s_pool, dgt)] + gw)
                    first_exp = False
                    extok[(c, qa)] = (s_act, t)
                    st["diag_exp"] = t
                    last_exp = t
                    if ncols > 128:
                        t = pg.op("scalar",
                                  lambda e, pslot=pslot, soff=soff,
                                  ncols=ncols, buf=buf, bcol=bcol:
                                  e.activation(
                                      probsT[:, pslot + 128: pslot + ncols],
                                      spanb[buf][:, soff + 128: soff + ncols],
                                      AF.Exp, bias=cst[:, bcol:bcol + 1]),
                                  inc=s_act, waits=[(s_dve, ev)])
                        for q in range(qa + 1, q1 + 1):
                            extok[(c, q)] = (s_act, t)
                        last_exp = t
                else:
                    t = pg.op("scalar",
                              lambda e, pslot=pslot, soff=soff, ncols=ncols,
                              buf=buf, bcol=bcol: e.activation(
                                  probsT[:, pslot: pslot + ncols],
                                  spanb[buf][:, soff: soff + ncols],
                                  AF.Exp, bias=cst[:, bcol:bcol + 1]),
                              inc=s_act, waits=[(s_dve, ev)] + gw)
                    first_exp = False
                    qb2 = (q1 if (h in GROUP_HEADS) else qa)
                    for q in range(qa, qb2 + 1):
                        extok[(c, q)] = (s_act, t)
                    last_exp = t
            st["span_exp"][buf] = last_exp
        if c >= 1:
            emit_av(c - 1)
    emit_av(NT - 1)
    if h == NH - 1:
        flush_z()
    st["vaug_reader"] = s_pe.n
    st["probsT_guard"] = s_pe.n


def _plan_ffn(pg, seq, b, sb, sem, wf_tok, lora_tok, x2_d, out_d, prefix):
    s_pe, s_dve, s_act, s_pool = sem["pe"], sem["dve"], sem["act"], sem["pool"]
    s_dx, s_dw, s_dt, s_do = sem["dx"], sem["dw"], sem["dt"], sem["do"]
    s_dl, s_dc = sem["dl"], sem["dc"]
    s_wg = sem["wg"]
    cst, e8, idf, onesb = sb["cst"], sb["e8"], sb["idf"], sb["onesb"]
    idb = sb["idb"]

    apply_tok = [0, 0]
    xnb_rd = [0, 0]
    xnT2_rd = [0, 0]
    ptc_copy = [0, 0]
    ev_out = [0, 0]
    store_tok = [0, 0]
    gl_evac = 0
    gl_user = 0
    wts_evac = 0
    wrep_evac = 0
    xa_evac = 0
    wxa_tok = 0
    base_evac = [0, 0]
    xa_reader = 0
    wxa_reader = 0
    IOTA = cst[:, CST_IOTA8:CST_IOTA8 + 8]

    for i in range(NT):
        bi = i % 2
        if i < len(prefix):
            ld, t_apply = prefix[i]
        else:
            ld = pg.op("scalar",
                       lambda e, i=i, bi=bi: e.dma_start(
                           b["x2b"][bi][:, :], x2_d.ap()[i]),
                       inc=sem["cxl"][bi], amt=16,
                       waits=[(s_pool, apply_tok[bi]), (s_dve, ev_out[bi])])
            t_apply = _plan_ln(pg, b["x2b"][bi], b["xnb"][bi], b["xnb"][bi],
                               b["cc"], 8 * bi, sem, (sem["cxl"][bi], ld),
                               [(s_pe, xnb_rd[bi])], eng="gpsimd")
        apply_tok[bi] = t_apply
        trs = []
        last_tr = 0
        for g4 in range(4):
            pk = g4 % 2
            for j in range(4):
                dc = g4 * 4 + j
                w = []
                if j == 0:
                    w = [(s_pool, t_apply), (s_dve, ptc_copy[pk])]
                tk = pg.op("tensor",
                           lambda e, bi=bi, dc=dc, j=j, pk=pk: e.transpose(
                               b["ptc"][pk][:, j * 128:(j + 1) * 128],
                               b["xnb"][bi][:, dc * 128:(dc + 1) * 128],
                               idb[:, :]),
                           inc=(s_pe if j == 3 else None), waits=w)
                if j == 3:
                    last_tr = tk
            ct = pg.op("vector",
                       lambda e, bi=bi, g4=g4, pk=pk: e.tensor_copy(
                           b["xnT2"][bi][:, g4 * 512:(g4 + 1) * 512],
                           b["ptc"][pk][:, 0:512]),
                       inc=s_dve,
                       waits=[(s_pe, last_tr)] +
                             ([(s_pe, xnT2_rd[bi])] if g4 == 0 else []))
            ptc_copy[pk] = ct
            trs.append(ct)
        xnb_rd[bi] = last_tr

        for dc in range(NDC):
            w = [(s_dve, trs[dc // 4])]
            if dc == 0:
                w.append((s_dve, gl_evac))
                if i == 0:
                    w.append(lora_tok[0])
                    w.append(lora_tok[1])
            pg.op("tensor",
                  lambda e, bi=bi, dc=dc: e.matmul(
                      b["pgl"][:, 0:8],
                      b["xnT2"][bi][:, dc * 128:(dc + 1) * 128],
                      b["gwt"][:, dc * 8:(dc + 1) * 8],
                      start=(dc == 0), stop=False),
                  inc=None, waits=w)
        glmm = pg.op("tensor",
                     lambda e: e.matmul(b["pgl"][:, 0:8], onesb[0:1, :],
                                        b["gbr"][0:1, 0:8],
                                        start=False, stop=True),
                     inc=s_pe)
        gl_evac = pg.op("vector",
                        lambda e: e.tensor_copy(b["gl"][:, :], b["pgl"][:, 0:8]),
                        inc=s_dve, waits=[(s_pe, glmm), (s_dve, gl_user)])
        xamm = 0
        for dc in range(NDC):
            w = [(s_dve, trs[dc // 4])]
            if dc == 0:
                w.append((s_dve, xa_evac))
                w.append((s_pe, xa_reader))
                if i == 0:
                    w.append(lora_tok[2])
            tk = pg.op("tensor",
                       lambda e, bi=bi, dc=dc: e.matmul(
                           b["pxa"][:, :],
                           b["at"][:, dc * 128:(dc + 1) * 128],
                           b["xnT2"][bi][:, dc * 128:(dc + 1) * 128],
                           start=(dc == 0), stop=(dc == NDC - 1)),
                       inc=(s_pe if dc == NDC - 1 else None), waits=w)
            if dc == NDC - 1:
                xamm = tk
        xa_reader = xamm
        xa_evac = pg.op("vector",
                        lambda e: e.tensor_copy(b["xa"][:, :], b["pxa"][:, :]),
                        inc=s_dve, waits=[(s_pe, xamm), (s_dve, wxa_tok)])

        g = b["gsc"]
        t = pg.op("vector", lambda e: e.max(g[:, 0:8], b["gl"][:, :]),
                  inc=s_dve, waits=[(s_dve, gl_evac)])
        t = pg.op("vector",
                  lambda e: e.tensor_scalar(g[:, 8:16], b["gl"][:, :],
                                            g[:, 0:1], None, ALU.is_equal),
                  inc=s_dve, waits=[(s_dve, t)])
        t = pg.op("vector",
                  lambda e: e.tensor_tensor(g[:, 40:48], g[:, 8:16], IOTA,
                                            ALU.mult),
                  inc=s_dve, waits=[(s_dve, t)])
        t = pg.op("vector",
                  lambda e: e.tensor_scalar(g[:, 48:56], g[:, 8:16],
                                            -1e9, 1e9, ALU.mult, ALU.add),
                  inc=s_dve, waits=[(s_dve, t)])
        t = pg.op("vector",
                  lambda e: e.tensor_tensor(g[:, 24:32], g[:, 40:48],
                                            g[:, 48:56], ALU.add),
                  inc=s_dve, waits=[(s_dve, t)])
        t = pg.op("vector",
                  lambda e: e.tensor_reduce(g[:, 32:33], g[:, 24:32],
                                            AX.X, ALU.min),
                  inc=s_dve, waits=[(s_dve, t)])
        t = pg.op("vector",
                  lambda e: e.tensor_scalar(g[:, 8:16], IOTA,
                                            g[:, 32:33], None, ALU.is_equal),
                  inc=s_dve, waits=[(s_dve, t)])
        t = pg.op("vector",
                  lambda e: e.tensor_scalar(g[:, 16:24], b["gl"][:, :],
                                            g[:, 1:2], None, ALU.is_equal),
                  inc=s_dve, waits=[(s_dve, t)])
        gl_user = t
        t = pg.op("vector",
                  lambda e: e.tensor_scalar(g[:, 24:32], g[:, 8:16],
                                            -1.0, 1.0, ALU.mult, ALU.add),
                  inc=s_dve, waits=[(s_dve, t)])
        t = pg.op("vector",
                  lambda e: e.tensor_tensor(g[:, 40:48], g[:, 16:24],
                                            g[:, 24:32], ALU.mult),
                  inc=s_dve, waits=[(s_dve, t)])
        t = pg.op("vector",
                  lambda e: e.tensor_scalar(g[:, 48:56], g[:, 40:48],
                                            -1e9, 1e9, ALU.mult, ALU.add),
                  inc=s_dve, waits=[(s_dve, t)])
        t = pg.op("vector",
                  lambda e: e.tensor_tensor(g[:, 24:32], g[:, 40:48], IOTA,
                                            ALU.mult),
                  inc=s_dve, waits=[(s_dve, t)])
        t = pg.op("vector",
                  lambda e: e.tensor_tensor(g[:, 24:32], g[:, 24:32],
                                            g[:, 48:56], ALU.add),
                  inc=s_dve, waits=[(s_dve, t)])
        t = pg.op("vector",
                  lambda e: e.tensor_reduce(g[:, 33:34], g[:, 24:32],
                                            AX.X, ALU.min),
                  inc=s_dve, waits=[(s_dve, t)])
        t = pg.op("vector",
                  lambda e: e.tensor_scalar(g[:, 16:24], IOTA,
                                            g[:, 33:34], None, ALU.is_equal),
                  inc=s_dve, waits=[(s_dve, t)])
        t = pg.op("vector",
                  lambda e: e.tensor_tensor(g[:, 34:35], g[:, 1:2],
                                            g[:, 0:1], ALU.subtract),
                  inc=s_dve, waits=[(s_dve, t)])
        te = pg.op("scalar",
                   lambda e: e.activation(g[:, 35:36], g[:, 34:35], AF.Exp),
                   inc=s_act, waits=[(s_dve, t)])
        t = pg.op("vector",
                  lambda e: e.tensor_scalar(g[:, 36:37], g[:, 35:36],
                                            1.0, None, ALU.add),
                  inc=s_dve, waits=[(s_act, te)])
        t = pg.op("vector",
                  lambda e: e.reciprocal(g[:, 37:38], g[:, 36:37]),
                  inc=s_dve, waits=[(s_dve, t)])
        t = pg.op("vector",
                  lambda e: e.tensor_tensor(g[:, 38:39], g[:, 35:36],
                                            g[:, 37:38], ALU.mult),
                  inc=s_dve, waits=[(s_dve, t)])
        t = pg.op("vector",
                  lambda e: e.tensor_scalar(g[:, 40:48], g[:, 8:16],
                                            g[:, 37:38], None, ALU.mult),
                  inc=s_dve, waits=[(s_dve, t)])
        t = pg.op("vector",
                  lambda e: e.tensor_scalar(g[:, 48:56], g[:, 16:24],
                                            g[:, 38:39], None, ALU.mult),
                  inc=s_dve, waits=[(s_dve, t)])
        wv_tok = pg.op("vector",
                       lambda e: e.tensor_tensor(g[:, 16:24], g[:, 40:48],
                                                 g[:, 48:56], ALU.add),
                       inc=s_dve, waits=[(s_dve, t)])
        ev4 = [None] * 4
        mm = 0

        def base_chunks(og):
            bk = og % 2
            m = 0
            for dc in range(NDC):
                w = [(s_dve, trs[dc // 4])]
                if og == 0 and dc % 2 == 0:
                    w.append((s_wg[dc // 2], wf_tok[(dc // 2) * 2 + 1]))
                if dc == 0:
                    w.append((s_dve, base_evac[bk]))
                tk = pg.op("tensor",
                           lambda e, bi=bi, dc=dc, og=og, bk=bk: e.matmul(
                               b["pb"][bk][:, :],
                               b["xnT2"][bi][:, dc * 128:(dc + 1) * 128],
                               b["wfT"][:, dc * D + og * 512:
                                        dc * D + (og + 1) * 512],
                               start=(dc == 0), stop=False),
                           inc=(s_pe if dc == NDC - 1 else None), waits=w)
                if dc == NDC - 1:
                    m = tk
            return m

        base_chunks(0)
        base_chunks(1)

        twt = pg.op("tensor",
                    lambda e: e.transpose(b["pwt"][:, :], g[:, 16:24],
                                          idf[:, :]),
                    inc=s_pe, waits=[(s_dve, wv_tok), (s_dve, wts_evac)])
        wts_evac = pg.op("vector",
                         lambda e: e.tensor_copy(b["wts"][:, :],
                                                 b["pwt"][:, :]),
                         inc=s_dve, waits=[(s_pe, twt)])
        twr = pg.op("tensor",
                    lambda e: e.matmul(b["pwr"][:, :], e8[:, :],
                                       b["wts"][0:8, :],
                                       start=True, stop=True),
                    inc=s_pe, waits=[(s_dve, wts_evac), (s_dve, wrep_evac)])
        wrep_evac = pg.op("vector",
                          lambda e: e.tensor_copy(b["wrs"][:, :],
                                                  b["pwr"][:, :]),
                          inc=s_dve, waits=[(s_pe, twr)])
        wxa_tok = pg.op("vector",
                        lambda e: e.tensor_tensor(b["wxa"][:, :], b["xa"][:, :],
                                                  b["wrs"][:, :], ALU.mult),
                        inc=s_dve,
                        waits=[(s_dve, wrep_evac), (s_dve, xa_evac),
                               (s_pe, wxa_reader)])

        def finish_og(og, first_delta):
            bk = og % 2
            m = pg.op("tensor",
                      lambda e, og=og, bk=bk: e.matmul(
                          b["pb"][bk][:, :], onesb[0:1, :],
                          b["fbr"][0:1, og * 512:(og + 1) * 512],
                          start=False, stop=False),
                      inc=s_pe,
                      waits=[lora_tok[4]] if (i == 0 and og == 0) else [])
            m = pg.op("tensor",
                      lambda e, og=og, bk=bk: e.matmul(
                          b["pb"][bk][:, :], b["wxa"][:, :],
                          b["bt"][:, og * 512:(og + 1) * 512],
                          start=False, stop=True),
                      inc=s_pe,
                      waits=([(s_dve, wxa_tok)] if first_delta else []) +
                            ([lora_tok[3]] if (i == 0 and og == 0) else []))
            ev = pg.op("vector",
                       lambda e, bi=bi, og=og, bk=bk: e.tensor_tensor(
                           b["outb"][bi][:, og * 512:(og + 1) * 512],
                           b["pb"][bk][:, :],
                           b["x2b"][bi][:, og * 512:(og + 1) * 512], ALU.add),
                       inc=s_dve,
                       waits=[(s_pe, m), (sem["dob"][bi], store_tok[bi])])
            base_evac[bk] = ev
            ev4[og] = ev
            return m

        finish_og(0, True)
        finish_og(1, False)
        base_chunks(2)
        base_chunks(3)
        finish_og(2, False)
        mm = finish_og(3, False)
        wxa_reader = mm
        xnT2_rd[bi] = mm
        ev_out[bi] = ev4[-1]
        store_tok[bi] = pg.op("sync",
                              lambda e, i=i, bi=bi: e.dma_start(
                                  out_d.ap()[seq][i], b["outb"][bi][:, :]),
                              inc=sem["dob"][bi], amt=16,
                              waits=[(s_dve, t_) for t_ in ev4])


def _host_consts():
    cst = np.zeros((P, CST_NCOL), dtype=np.float32)
    p = np.arange(P)
    cst[:, CST_CAUSALT:CST_CAUSALT + 128] = np.where(
        p[:, None] > p[None, :], NEG, 0.0)
    cst[:, CST_CAUSALTM:CST_CAUSALTM + 128] = np.where(
        p[None, :] > p[:, None], NEG, 0.0)
    for h in EXACT_HEADS:
        sl = SLOPES[h]
        for d in range(DK[h] + 1):
            for u in range(2):
                cst[:, _t0_off[h] + 2 * d + u] = sl * (p - 63 - 128 * d - 64 * u)
    for h in PERQT_HEADS:
        sl = SLOPES[h]
        for d in range(DK[h] + 1):
            cst[:, _t1_off[h] + d] = sl * (p - 127 - 128 * d)
    for h in GROUP_HEADS:
        sl = SLOPES[h]
        for gd in range(-3, 13):
            cst[:, _t2_off[h] + gd + 3] = sl * (p - 511 - 128 * gd)
    cst[:, CST_IOTA8:CST_IOTA8 + 8] = np.arange(8)[None, :]
    cst[:, CST_EPS] = 1e-5
    cst[:, CST_MHALF] = -0.5
    return cst


def kernel(L0_attn_resid_pre, L0_attn_Wq, L0_attn_Wk, L0_attn_Wv, L0_attn_Wo,
           L0_ffn_W, L0_ffn_b, L0_ffn_lora_A, L0_ffn_lora_B, L0_ffn_gate_w,
           L0_ffn_gate_b):
    f32 = np.float32
    x = np.ascontiguousarray(np.asarray(L0_attn_resid_pre, dtype=f32))
    Wq = np.asarray(L0_attn_Wq, dtype=f32)
    Wk = np.asarray(L0_attn_Wk, dtype=f32)
    Wv = np.asarray(L0_attn_Wv, dtype=f32)
    Wo = np.asarray(L0_attn_Wo, dtype=f32)
    Wf = np.asarray(L0_ffn_W, dtype=f32)
    fb = np.asarray(L0_ffn_b, dtype=f32)
    la = np.asarray(L0_ffn_lora_A, dtype=f32)
    lb = np.asarray(L0_ffn_lora_B, dtype=f32)
    gw = np.asarray(L0_ffn_gate_w, dtype=f32)
    gb = np.asarray(L0_ffn_gate_b, dtype=f32)

    def tile_qkv(W):
        return np.ascontiguousarray(
            W.reshape(NH, DH, NDC, P).transpose(0, 3, 2, 1))

    common = {
        "wq": tile_qkv(Wq), "wk": tile_qkv(Wk), "wv": tile_qkv(Wv),
        "wo": np.ascontiguousarray(Wo.T.reshape(NDC, P, D)),
        "wf": np.ascontiguousarray(Wf.T.reshape(NDC, P, D)),
        "fb": np.ascontiguousarray(fb.reshape(1, D)),
        "la": np.ascontiguousarray(la.transpose(2, 0, 1).reshape(NDC, P, 128)),
        "lb": np.ascontiguousarray(lb.transpose(0, 2, 1).reshape(128, D)),
        "gw": np.ascontiguousarray(gw.T.reshape(NDC, P, 8)),
        "gb": np.ascontiguousarray(gb.reshape(1, 8)),
        "arep": np.ascontiguousarray(np.broadcast_to(
            (np.arange(512, dtype=f32)[None, :] *
             np.asarray(SLOPES[:2], dtype=f32)[:, None])[:, None, :],
            (2, P, 512))),
        "cst": _host_consts(),
        "e8": np.ascontiguousarray(
            (np.arange(8)[:, None] == (np.arange(128)[None, :] // 16))
            .astype(f32)),
        "idf": np.eye(P, dtype=f32),
    }

    nc = build_program()
    in_maps = []
    for c in range(NCORES):
        m = dict(common)
        m["x"] = np.ascontiguousarray(
            x[2 * c:2 * c + 2].reshape(NSEQ, NT, P, D))
        in_maps.append(m)

    res = run_bass_kernel_spmd(nc, in_maps, core_ids=list(range(NCORES)))
    out = np.concatenate(
        [r["out"].reshape(NSEQ, SEQL, D) for r in res.results], axis=0)
    return out
